# version 06
# baseline (speedup 1.0000x reference)
import contextlib
import numpy as np
import ml_dtypes
import concourse.bass as bass
import concourse.mybir as mybir
from concourse.bass_utils import run_bass_kernel_spmd

F32 = mybir.dt.float32
BF16 = mybir.dt.bfloat16
I32 = mybir.dt.int32
AX = mybir.AxisListType.X
ALU = mybir.AluOpType
ACT = mybir.ActivationFunctionType
BF = ml_dtypes.bfloat16

T, D, DH, H, HS, NF = 2048, 1024, 4096, 16, 64, 4
NT, NCH, BL = 16, 4, 2
EPS = 1e-5

PHID = [(h % 2) * 8 + h // 2 for h in range(H)]


_SIG = {}


class Ctr:
    def __init__(self, es, nc, name):
        self.h = es.enter_context(nc.semaphore(name))
        self._n = 0
        self.last = None
        self.wmap = {}

    @property
    def n(self):
        if self.last is not None and self.last not in _SIG:
            last, self.last = self.last, None
            self.inc(last)
            self.last = last
        return self._n

    def inc(self, inst, k=1):
        inst.then_inc(self.h, k)
        self._n += k
        _SIG[inst] = (self, self._n)
        return inst

    def dma(self, inst):
        return self.inc(inst, 16)


class SerEng:
    __slots__ = ("e", "c", "dctrs")
    _PASS = frozenset(["wait_ge", "then_inc"])

    def __init__(self, e, c, dctrs=()):
        object.__setattr__(self, "e", e)
        object.__setattr__(self, "c", c)
        object.__setattr__(self, "dctrs", dctrs)

    def dma_start(self, *args, **kw):
        e = self.e
        c = self.c
        for d in self.dctrs:
            v = d.n
            key = ("dma", d)
            wm = (c.wmap if c is not None else d.wmap)
            if v > wm.get(key, 0):
                e.wait_ge(d.h, v)
                wm[key] = v
        return e.dma_start(*args, **kw)

    def __getattr__(self, name):
        a = getattr(self.e, name)
        if name in SerEng._PASS or not callable(a) or self.c is None:
            return a
        c = self.c
        e = self.e

        def w(*args, **kw):
            prev = c.last
            if prev is not None:
                sig = _SIG.get(prev)
                if sig is None:
                    c.inc(prev)
                    sig = (c, c.n)
                ctr2, val = sig
                if val > c.wmap.get(ctr2, 0):
                    e.wait_ge(ctr2.h, val)
                    c.wmap[ctr2] = val
            inst = a(*args, **kw)
            c.last = inst
            return inst
        return w


def _consts():
    pp, cc = np.meshgrid(np.arange(128), np.arange(896), indexing="ij")
    I_all = np.zeros((128, 128), dtype=BF)
    IT_all = np.zeros((16, 1024), dtype=BF)
    for pt in range(8):
        for half in range(2):
            hrow = PHID[2 * pt + half]
            I_all[64 * half:64 * half + 64, 16 * pt + hrow] = 1
            IT_all[hrow, 128 * pt + 64 * half:128 * pt + 64 * half + 64] = 1
    selpair = np.zeros((16, 1024), dtype=np.float32)
    for o in range(8):
        selpair[PHID[2 * o], 128 * o:128 * o + 64] = 1
        selpair[PHID[2 * o + 1], 128 * o + 64:128 * o + 128] = 1
    return dict(
        ident16=np.eye(128, dtype=BF),
        identf=np.eye(128, dtype=np.float32),
        bigmask=(pp <= cc - 384).astype(BF),
        iall=I_all, itall=IT_all, selpair=selpair,
        onesf=np.ones((1, 128), dtype=np.float32),
        fconst=np.tile(np.array([[1e-5, 1e-10, -0.5, 0.125, 1.0 + 1e-5, 0.0, 1.0, 2.0]],
                                dtype=np.float32), (128, 1)),
        onescol_b=np.ones((128, 1), dtype=BF),
        ones128f=np.ones((128, 1), dtype=np.float32),
    )


def build_nc():
    _SIG.clear()
    nc = bass.Bass("TRN2", target_bir_lowering=False)

    def din(name, shape):
        return nc.dram_tensor(name, shape, F32, kind="ExternalInput").ap()

    x_in = din("x", [BL, T, D])
    wqkvf = din("wqkvf", [4 * D, D])
    wconv = din("wconv", [HS, HS, NF])
    wproj = din("wproj", [D, D])
    wff1 = din("wff1", [D, DH])
    wff2 = din("wff2", [DH, D])
    ln1g = din("ln1g", [D]); ln1b = din("ln1b", [D])
    ln2g = din("ln2g", [D]); ln2b = din("ln2b", [D])
    w1 = din("w1", [DH, D])
    convw = din("convw", [DH, 1, 3])
    convb = din("convb", [DH])
    w2 = din("w2", [D, DH])
    out_d = nc.dram_tensor("out", [BL, T, D], F32, kind="ExternalOutput").ap()

    def scr(name, shape, dt):
        return nc.dram_tensor(name, shape, dt, kind="Internal").ap()

    wqkvfT_s = scr("wqkvfT_s", [32, 8, 128, 128], BF16)
    wprojT_s = scr("wprojT_s", [8, 128, 1024], BF16)
    m1T_s = scr("m1T_s", [32, 8, 128, 128], BF16)
    m2T_s = scr("m2T_s", [32, 128, 1024], BF16)
    qT_s = scr("qT_s", [8, 128, T], BF16)
    fT_s = scr("fT_s", [8, 128, T], BF16)
    u_s = scr("u_s", [NT, 128, D], F32)
    u1_s = scr("u1_s", [NT, 128, D], F32)
    xa_s = scr("xa_s", [NT, 128, D], F32)
    h2w_s = scr("h2w_s", [32, 128, T], F32)
    n2_s = scr("n2_s", [32, 128, T], BF16)
    dct_s = scr("dct_s", [2, 1, 4096], F32)

    cn = _consts()
    cd = {k: nc.inline_tensor(v, k + "_c") for k, v in cn.items()}

    es = contextlib.ExitStack()

    def sb(name, shape, dt=F32):
        return es.enter_context(nc.sbuf_tensor(name, shape, dt))

    ident16 = sb("ident16", [128, 128], BF16)
    identf = sb("identf", [128, 128], F32)
    bigmask = sb("bigmask", [128, 896], BF16)
    iall = sb("iall", [128, 128], BF16)
    itall = sb("itall", [16, 1024], BF16)
    selpair = sb("selpair", [16, 1024], F32)
    onesf = sb("onesf", [1, 128], F32)
    onescol_b = sb("onescol_b", [128, 1], BF16)
    ones128f = sb("ones128f", [128, 1], F32)
    convb_pp = sb("convb_pp", [128, 32], F32)
    convw_pp = sb("convw_pp", [128, 96], F32)
    bdw = sb("bdw", [128, 512], BF16)
    wconv_sb = sb("wconv_sb", [128, 256], F32)
    cvstage = sb("cvstage", [128, 512], F32)
    lnrow = sb("lnrow", [1, D], F32)
    gb_bc = sb("gb_bc", [128, 2 * D], F32)
    meanw = sb("meanw", [128, 2], F32)
    kmax2 = sb("kmax2", [16, 1], F32)
    stats = sb("stats", [128, 8], F32)
    smallA = sb("smallA", [128, 64], F32)
    smallB = sb("smallB", [128, 64], F32)
    tbufA = sb("tbufA", [128, 128], F32)
    fconst = sb("fconst", [128, 8], F32)
    for _i, _v in enumerate([float(EPS), float(EPS * EPS), -0.5, 0.125, float(1.0 + EPS)]):
        nc.const_aps.register(F32, float(_v), fconst[:, _i:_i + 1])
    gam1 = sb("gam1", [128, 16], F32)
    gam2 = sb("gam2", [128, 16], F32)

    dS = Ctr(es, nc, "dS"); dG = Ctr(es, nc, "dG"); dC = Ctr(es, nc, "dC")
    dT = Ctr(es, nc, "dT")
    cT = Ctr(es, nc, "cT"); cV = Ctr(es, nc, "cV"); cS = Ctr(es, nc, "cS")
    cG = Ctr(es, nc, "cG")

    _uid = [0]

    def _un(name):
        _uid[0] += 1
        return f"{name}_{_uid[0]}"

    blk = es.enter_context(nc.Block())

    _eng_ctr = {"tensor": (cT, ()), "vector": (cV, ()),
                "scalar": (cS, (dC,)), "gpsimd": (cG, (dG,)),
                "sync": (None, (dS, dT))}

    def on(eng, fn):
        c, dctrs = _eng_ctr[eng]
        getattr(blk, eng)(lambda e: fn(SerEng(e, c, dctrs)))

    def VE(v):
        cV.inc(v.engine_nop())
        v.wait_ge(cV.h, cV.n)

    def barrier():
        for c in (cT, cV, cS, cG):
            if c.last is not None and c.last not in _SIG:
                c.inc(c.last)
        vals = [(dS, dS.n), (dG, dG.n), (dC, dC.n), (dT, dT.n), (cT, cT.n),
                (cV, cV.n), (cS, cS.n), (cG, cG.n)]

        def fb(e, vals=vals):
            for ctr, val in vals:
                e.wait_ge(ctr.h, val)
        for eng in ["sync", "gpsimd", "tensor", "vector", "scalar"]:
            on(eng, fb)
            c, dctrs = _eng_ctr[eng]
            wm = c.wmap if c is not None else dctrs[0].wmap
            for ctr, val in vals:
                wm[ctr] = max(wm.get(ctr, 0), val)
            for d in dctrs:
                dwm = c.wmap if c is not None else d.wmap
                dwm[("dma", d)] = max(dwm.get(("dma", d), 0), d.n)

    def s0_consts():
        def f(sync):
            for dst, key in [(ident16, "ident16"), (identf, "identf"),
                             (bigmask, "bigmask"), (iall, "iall"), (itall, "itall"),
                             (selpair, "selpair"), (onesf, "onesf"),
                             (fconst, "fconst"),
                             (onescol_b, "onescol_b"), (ones128f, "ones128f")]:
                dS.dma(sync.dma_start(out=dst[:], in_=cd[key].ap()))
            dS.dma(sync.dma_start(out=wconv_sb[0:64, :],
                                  in_=wconv.rearrange("c d f -> c (d f)")))
            dS.dma(sync.dma_start(out=wconv_sb[64:128, :],
                                  in_=wconv.rearrange("c d f -> c (d f)")))
            dS.dma(sync.dma_start(out=cvstage[0:32, 0:128],
                                  in_=convb.rearrange("(j p) -> j p", p=128)))
            dS.dma(sync.dma_start(
                out=cvstage[0:32, 128:512],
                in_=convw.rearrange("(j p) o k -> j (p o k)", p=128)))

        on("sync", f)
        ld0 = dS.n
        ps0es = contextlib.ExitStack()
        ps0 = ps0es.enter_context(nc.psum_tensor("ps0c", [128, 128], F32))

        def ftc0(t):
            t.wait_ge(dS.h, ld0)
            cT.inc(t.transpose(ps0[:, 0:32], cvstage[0:32, 0:128], identf[0:32, 0:32]))
        on("tensor", ftc0)
        tcn = cT.n

        def fvc0(v, tcn=tcn):
            v.wait_ge(cT.h, tcn)
            v.tensor_copy(convb_pp[:], ps0[:, 0:32])
            cV.inc(v.engine_nop())
        on("vector", fvc0)
        vcn = cV.n
        for k3 in range(3):
            def ftc1(t, k3=k3, vcn=vcn):
                t.wait_ge(cV.h, vcn)
                cT.inc(t.transpose(
                    ps0[:, 0:32],
                    cvstage[0:32, 128 + k3:512:3], identf[0:32, 0:32]))
            on("tensor", ftc1)
            tcn = cT.n

            def fvc1(v, k3=k3, tcn=tcn):
                v.wait_ge(cT.h, tcn)
                v.tensor_copy(convw_pp[:].rearrange("p (j k) -> p k j", k=3)[:, k3],
                              ps0[:, 0:32])
                cV.inc(v.engine_nop())
            on("vector", fvc1)
            vcn = cV.n

        def fv(v):
            v.wait_ge(dS.h, ld0)
            v.memset(bdw[:], 0.0)
            for fi in range(NF):
                v.tensor_copy(bdw[0:64, 128 * fi:128 * fi + 64],
                              wconv_sb[0:64, fi:256:4])
                v.tensor_copy(bdw[64:128, 128 * fi + 64:128 * fi + 128],
                              wconv_sb[64:128, fi:256:4])
            cV.inc(v.engine_nop())
        on("vector", fv)
        ps0es.close()

    def s0_wprep():
        with contextlib.ExitStack() as ps:
            def psb(name, shape, dt=F32):
                return ps.enter_context(nc.sbuf_tensor(_un(name), shape, dt))
            wrow = psb("p0_wrow", [128, 4096], F32)
            wtmp = psb("p0_wtmp", [128, 4096], F32)
            wi32 = psb("p0_wi32", [128, 4096], I32)
            wbf = psb("p0_wbf", [128, 4096], BF16)
            stage = psb("p0_stage", [128, 4096], BF16)
            acc = psb("p0_acc", [128, 2], F32)
            psT = ps.enter_context(nc.psum_tensor(_un("ps0T"), [128, 1024], BF16))
            psF = ps.enter_context(nc.psum_tensor(_un("ps0F"), [128, 512], F32))

            for col, (w_hbm, ntile, width) in enumerate([(w1, 32, D), (w2, 8, DH)]):
                for rt in range(ntile):
                    vn = cV.n

                    def fs(sync, rt=rt, w_hbm=w_hbm, width=width, vn=vn):
                        sync.wait_ge(cV.h, vn)
                        dS.dma(sync.dma_start(out=wrow[:, 0:width],
                                              in_=w_hbm[128 * rt:128 * rt + 128, :]))
                    on("sync", fs)
                    ldn = dS.n

                    def fv(v, rt=rt, width=width, col=col, ldn=ldn):
                        v.wait_ge(dS.h, ldn)
                        v.reduce_sum(smallA[:, 0:1], wrow[:, 0:width], AX,
                                     apply_absolute_value=True)
                        VE(v)
                        if rt == 0:
                            v.tensor_copy(acc[:, col:col + 1], smallA[:, 0:1])
                        else:
                            v.tensor_add(acc[:, col:col + 1], acc[:, col:col + 1],
                                         smallA[:, 0:1])
                        cV.inc(v.engine_nop())
                    on("vector", fv)
            vn = cV.n

            def ft(t, vn=vn):
                t.wait_ge(cV.h, vn)
                cT.inc(t.matmul(psF[0:2, 0:1], acc[:], ones128f[:],
                                start=True, stop=True))
            on("tensor", ft)
            tn = cT.n

            def fv2(v, tn=tn):
                v.wait_ge(cT.h, tn)
                v.tensor_scalar(smallA[0:2, 0:1], psF[0:2, 0:1],
                                float(1.0 / (DH * D)), 1e-5, ALU.mult, ALU.max)
                v.memset(tbufA[:], 0.0)
                VE(v)
                v.tensor_copy(tbufA[0:2, 0:1], smallA[0:2, 0:1])
                cV.inc(v.engine_nop())
            on("vector", fv2)
            vn = cV.n

            def ft2(t, vn=vn):
                t.wait_ge(cV.h, vn)
                cT.inc(t.transpose(psF[:, 0:128], tbufA[:], identf[:]))
            on("tensor", ft2)
            tn = cT.n

            def fv3(v, tn=tn):
                v.wait_ge(cT.h, tn)
                v.tensor_copy(smallB[0:1, 0:2], psF[0:1, 0:2])
                cV.inc(v.engine_nop())
            on("vector", fv3)
            vn = cV.n

            def ft3(t, vn=vn):
                t.wait_ge(cV.h, vn)
                cT.inc(t.matmul(psF[:, 0:2], onesf[:], smallB[0:1, 0:2],
                                start=True, stop=True))
            on("tensor", ft3)
            tn = cT.n

            def fv4(v, tn=tn):
                v.wait_ge(cT.h, tn)
                v.tensor_copy(meanw[:], psF[:, 0:2])
                VE(v)
                v.reciprocal(smallB[:, 2:4], meanw[:])
                cV.inc(v.engine_nop())
            on("vector", fv4)
            recip_ready = cV.n

            jobs = [("m1", w1, 32, D, None), ("m2", w2, 8, DH, None),
                    ("qkvf", wqkvf, 32, D, None), ("proj", wproj, 8, D, None)]
            stage_rd = [0]
            for which, w_hbm, ntile, width, _ in jobs:
                tern = which in ("m1", "m2")
                for rt in range(ntile):
                    vn = cV.n
                    if tern:
                        def fs(sync, rt=rt, w_hbm=w_hbm, width=width, vn=vn):
                            sync.wait_ge(cV.h, vn)
                            dS.dma(sync.dma_start(
                                out=wrow[:, 0:width],
                                in_=w_hbm[128 * rt:128 * rt + 128, :]))
                        on("sync", fs)
                        ldn = dS.n
                        col = 0 if which == "m1" else 1

                        def fv(v, width=width, col=col, ldn=ldn):
                            v.wait_ge(dS.h, ldn)
                            v.wait_ge(cV.h, recip_ready)
                            v.tensor_scalar_mul(wtmp[:, 0:width], wrow[:, 0:width],
                                                smallB[:, 2 + col:3 + col])
                            v.tensor_copy(wi32[:, 0:width], wtmp[:, 0:width])
                            v.tensor_scalar(wbf[:, 0:width], wi32[:, 0:width],
                                            -1.0, 1.0, ALU.max, ALU.min)
                            cV.inc(v.engine_nop())
                        on("vector", fv)
                    else:
                        def fg(g, rt=rt, w_hbm=w_hbm, width=width, vn=vn):
                            g.wait_ge(cV.h, vn)
                            dG.dma(g.dma_start(out=wbf[:, 0:width],
                                               in_=w_hbm[128 * rt:128 * rt + 128, :]))
                        on("gpsimd", fg)
                        ldn = dG.n

                        def fv(v, ldn=ldn):
                            v.wait_ge(dG.h, ldn)
                            cV.inc(v.engine_nop())
                        on("vector", fv)
                    ngrp = width // 1024
                    for g in range(ngrp):
                        vn = cV.n

                        def ftr(t, g=g, vn=vn):
                            t.wait_ge(cV.h, vn)
                            for j in range(8):
                                _li = t.transpose(
                                    psT[:, 128 * j:128 * j + 128],
                                    wbf[:, 1024 * g + 128 * j:1024 * g + 128 * j + 128],
                                    ident16[:])
                            cT.inc(_li)
                        on("tensor", ftr)
                        tn = cT.n

                        def fev(v, g=g, tn=tn, srd=stage_rd[0]):
                            v.wait_ge(cT.h, tn)
                            v.wait_ge(dT.h, srd)
                            v.tensor_copy(stage[:, 1024 * g:1024 * g + 1024], psT[:])
                            cV.inc(v.engine_nop())
                        on("vector", fev)
                    sn = cV.n

                    def fst(sync, rt=rt, which=which, sn=sn):
                        sync.wait_ge(cV.h, sn)
                        if which == "m1":
                            dT.dma(sync.dma_start(
                                out=m1T_s[rt].rearrange("j p c -> p j c"),
                                in_=stage[:, 0:1024].rearrange("p (j c) -> p j c", j=8)))
                        elif which == "m2":
                            dT.dma(sync.dma_start(
                                out=m2T_s[:, :, 128 * rt:128 * rt + 128]
                                .rearrange("j p o -> p j o"),
                                in_=stage[:].rearrange("p (j o) -> p j o", j=32)))
                        elif which == "qkvf":
                            dT.dma(sync.dma_start(
                                out=wqkvfT_s[rt].rearrange("j p c -> p j c"),
                                in_=stage[:, 0:1024].rearrange("p (j c) -> p j c", j=8)))
                        else:
                            dT.dma(sync.dma_start(
                                out=wprojT_s[:, :, 128 * rt:128 * rt + 128]
                                .rearrange("j p o -> p j o"),
                                in_=stage[:, 0:1024].rearrange("p (j o) -> p j o", j=8)))
                    on("sync", fst)
                    stage_rd[0] = dT.n

    def s1(b, ktT, vbuf):
        with contextlib.ExitStack() as ps:
            def psb(name, shape, dt=F32):
                return ps.enter_context(nc.sbuf_tensor(_un(name), shape, dt))
            h1T = psb("s1_h1T", [128, 8 * T], BF16)
            xt = psb("s1_xt", [128, 2 * D], F32)
            sq = psb("s1_sq", [128, D], F32)
            ut = psb("s1_ut", [128, D], F32)
            ta = psb("s1_ta", [128, D], F32)
            h1f = psb("s1_h1f", [128, D], F32)
            wring = psb("s1_wring", [128, 2 * 1024], BF16)
            wvfull = psb("s1_wvfull", [128, 8 * 1024], BF16)
            qfst = psb("s1_qfst", [128, 2 * 2048], BF16)
            sqk = psb("s1_sqk", [128, 2048], BF16)
            psW0 = ps.enter_context(nc.psum_tensor(_un("s1_psW0"), [128, 1024], F32))
            psW1 = ps.enter_context(nc.psum_tensor(_un("s1_psW1"), [128, 1024], F32))
            psTP = ps.enter_context(nc.psum_tensor(_un("s1_psTP"), [128, 1024], F32))
            psKN = ps.enter_context(nc.psum_tensor(_un("s1_psKN"), [16, 512], F32))

            for half, row_hbm in [(0, ln1g), (1, ln1b)]:
                def flr(sync, row_hbm=row_hbm, _tp=cT.n):
                    sync.wait_ge(cT.h, _tp)
                    dS.dma(sync.dma_start(out=lnrow[:], in_=row_hbm[None, :]))
                on("sync", flr)
                ldr = dS.n

                def ftg(t, half=half, ldr=ldr):
                    t.wait_ge(dS.h, ldr)
                    pdst = psW0 if half == 0 else psW1
                    for n in range(2):
                        cT.inc(t.matmul(
                            pdst[:, 512 * n:512 * n + 512], onesf[:],
                            lnrow[0:1, 512 * n:512 * n + 512],
                            start=True, stop=True))
                on("tensor", ftg)
            tn = cT.n

            def fvg(v, tn=tn):
                v.wait_ge(cT.h, tn)
                v.tensor_copy(gb_bc[:, 0:D], psW0[:])
                v.tensor_copy(gb_bc[:, D:2 * D], psW1[:])
                cV.inc(v.engine_nop())
            on("vector", fvg)
            gbn = cV.n

            ev_done = {}
            for i in range(NT):
                prevev = ev_done.get(i - 2, 0)

                def fs(sync, i=i, b=b, prevev=prevev):
                    sync.wait_ge(cV.h, prevev)
                    dS.dma(sync.dma_start(out=xt[:, D * (i % 2):D * (i % 2) + D],
                                          in_=x_in[b, 128 * i:128 * i + 128, :]))
                on("sync", fs)
                ldn = dS.n
                xs = xt[:, D * (i % 2):D * (i % 2) + D]

                def fv1(v, xs=xs, ldn=ldn, _sn=cS.n):
                    v.wait_ge(dS.h, ldn)
                    v.wait_ge(cS.h, _sn)
                    v.reduce_sum(stats[:, 0:1], xs, AX)
                    v.tensor_mul(sq[:], xs, xs)
                    v.reduce_sum(stats[:, 1:2], sq[:], AX)
                    v.tensor_scalar_mul(stats[:, 2:3], stats[:, 0:1], float(1.0 / D))
                    v.tensor_scalar_mul(stats[:, 3:4], stats[:, 1:2], float(1.0 / D))
                    VE(v)
                    v.tensor_mul(stats[:, 4:5], stats[:, 2:3], stats[:, 2:3])
                    VE(v)
                    v.tensor_sub(stats[:, 3:4], stats[:, 3:4], stats[:, 4:5])
                    cV.inc(v.engine_nop())
                on("vector", fv1)
                vn = cV.n

                def fs1(s, i=i, vn=vn):
                    s.wait_ge(cV.h, vn)
                    _li = s.activation(stats[:, 4:5], stats[:, 3:4], ACT.Ln, bias=float(EPS))
                    _li = s.activation(smallA[:, i:i + 1], stats[:, 4:5], ACT.Exp, scale=-0.5)
                    _li = s.activation(stats[:, 5:6], stats[:, 3:4], ACT.Ln,
                                 bias=float(EPS * EPS), scale=float(1.0 + EPS))
                    _li = s.activation(smallB[:, i:i + 1], stats[:, 5:6], ACT.Exp, scale=-0.5)
                    cS.inc(_li)
                on("scalar", fs1)
                sn = cS.n

                def fv2(v, i=i, sn=sn):
                    v.wait_ge(cS.h, sn)
                    v.scalar_tensor_tensor(smallA[:, 16 + i:17 + i], stats[:, 2:3],
                                           -1.0, smallA[:, i:i + 1], ALU.mult, ALU.mult)
                    v.scalar_tensor_tensor(smallB[:, 16 + i:17 + i], stats[:, 2:3],
                                           -1.0, smallB[:, i:i + 1], ALU.mult, ALU.mult)
                    cV.inc(v.engine_nop())
                on("vector", fv2)
                vn = cV.n

                def fv3(v, i=i, xs=xs, vn=vn, gbn=gbn, _tp=cT.n, _dp=dS.n):
                    v.wait_ge(cV.h, vn)
                    v.wait_ge(cV.h, gbn)
                    v.wait_ge(cT.h, _tp)
                    v.wait_ge(dS.h, _dp)
                    v.tensor_scalar(ut[:], xs, smallA[:, i:i + 1],
                                    smallA[:, 16 + i:17 + i], ALU.mult, ALU.add)
                    v.tensor_scalar(ta[:], xs, smallB[:, i:i + 1],
                                    smallB[:, 16 + i:17 + i], ALU.mult, ALU.add)
                    v.tensor_mul(ta[:], ta[:], gb_bc[:, 0:D])
                    v.tensor_add(h1f[:], ta[:], gb_bc[:, D:2 * D])
                    cV.inc(v.engine_nop())
                on("vector", fv3)
                vn = cV.n

                def fsu(sync, i=i, vn=vn):
                    sync.wait_ge(cV.h, vn)
                    dS.dma(sync.dma_start(out=u_s[i], in_=ut[:]))
                on("sync", fsu)
                prev_ev = ev_done.get(i - 1, 0)

                def ftr(t, vn=vn, prev_ev=prev_ev):
                    t.wait_ge(cV.h, vn)
                    t.wait_ge(cV.h, prev_ev)
                    for j in range(8):
                        _li = t.transpose(psTP[:, 128 * j:128 * j + 128],
                                    h1f[:, 128 * j:128 * j + 128], identf[:])
                    cT.inc(_li)
                on("tensor", ftr)
                tn = cT.n

                def fev(v, i=i, tn=tn):
                    v.wait_ge(cT.h, tn)
                    v.tensor_copy(
                        h1T[:].rearrange("p (j t) -> p j t", j=8)[:, :, 128 * i:128 * i + 128],
                        psTP[:].rearrange("p (j t) -> p j t", j=8))
                    cV.inc(v.engine_nop())
                    ev_done[i] = cV.n
                on("vector", fev)
            h1T_done = cV.n
            u_spill_done = dS.n

            ldprev = {}
            for grp, dst in [("q", None), ("k", None), ("f", None)]:
                base_o = {"q": 0, "k": 8, "f": 24}[grp]
                for o in range(8):
                    scn = cS.n

                    def fl(sync, o=o, base_o=base_o, scn=scn):
                        sync.wait_ge(cS.h, scn)
                        dS.dma(sync.dma_start(
                            out=wring[:, 1024 * (o % 2):1024 * (o % 2) + 1024]
                            .rearrange("p (k c) -> p k c", k=8),
                            in_=wqkvfT_s[base_o + o].rearrange("k p c -> p k c")))
                    on("sync", fl)
                    ldn = dS.n

                    def ft(t, o=o, ldn=ldn):
                        t.wait_ge(dS.h, ldn)
                        t.wait_ge(cV.h, h1T_done)
                        t.wait_ge(cS.h, cS.n)
                        for k in range(8):
                            lhsT = wring[:, 1024 * (o % 2) + 128 * k:
                                         1024 * (o % 2) + 128 * k + 128]
                            for n in range(4):
                                pdst = (psW0 if n < 2 else psW1)
                                _li = t.matmul(pdst[:, 512 * (n % 2):512 * (n % 2) + 512],
                                         lhsT,
                                         h1T[:, T * k + 512 * n:T * k + 512 * n + 512],
                                         start=(k == 0), stop=(k == 7))
                        cT.inc(_li)
                    on("tensor", ft)
                    tn = cT.n

                    def fe(s, o=o, grp=grp, tn=tn):
                        s.wait_ge(cT.h, tn)
                        if grp == "k":
                            _li = s.copy(ktT[:, T * o:T * o + 1024], psW0[:])
                            _li = s.copy(ktT[:, T * o + 1024:T * o + 2048], psW1[:])
                        else:
                            st = qfst[:, 2048 * (o % 2):2048 * (o % 2) + 2048]
                            _li = s.copy(st[:, 0:1024], psW0[:])
                            _li = s.copy(st[:, 1024:2048], psW1[:])
                        cS.inc(_li)
                    on("scalar", fe)
                    sn = cS.n
                    if grp != "k":
                        def fst(sync, o=o, grp=grp, sn=sn):
                            sync.wait_ge(cS.h, sn)
                            dst_s = qT_s if grp == "q" else fT_s
                            dS.dma(sync.dma_start(
                                out=dst_s[o],
                                in_=qfst[:, 2048 * (o % 2):2048 * (o % 2) + 2048]))
                        on("sync", fst)

            def flv(sync):
                for k in range(8):
                    dS.dma(sync.dma_start(
                        out=wvfull[:, 1024 * k:1024 * k + 1024]
                        .rearrange("p (j c) -> p j c", j=8),
                        in_=wqkvfT_s[16:24, k].rearrange("j p c -> p j c")))
            on("sync", flv)
            ldv = dS.n
            vones = cV.n

            def fvo(v):
                v.memset(vbuf[:].rearrange("p (ti h s) -> p ti h s", ti=NT, h=H)[:, :, :, 64:65], 1.0)
                cV.inc(v.engine_nop())
            on("vector", fvo)

            for ti in range(NT):
                scn = cS.n

                def ftv(t, ti=ti, scn=scn):
                    t.wait_ge(dS.h, ldv)
                    t.wait_ge(cV.h, h1T_done)
                    t.wait_ge(cS.h, scn)
                    for k in range(8):
                        lhsT = h1T[:, T * k + 128 * ti:T * k + 128 * ti + 128]
                        _li = t.matmul(psW0[:, 0:512], lhsT,
                                 wvfull[:, 1024 * k:1024 * k + 512],
                                 start=(k == 0), stop=(k == 7))
                        _li = t.matmul(psW0[:, 512:1024], lhsT,
                                 wvfull[:, 1024 * k + 512:1024 * k + 1024],
                                 start=(k == 0), stop=(k == 7))
                    cT.inc(_li)
                on("tensor", ftv)
                tn = cT.n

                def fve(s, ti=ti, tn=tn):
                    s.wait_ge(cT.h, tn)
                    _li = s.copy(vbuf[:].rearrange("p (ti h s) -> p ti h s", ti=NT, h=H)
                           [:, ti, :, 0:64],
                           psW0[:].rearrange("p (h s) -> p h s", h=H))
                    cS.inc(_li)
                on("scalar", fve)

            for pt in range(8):
                def fvk(v, pt=pt, _sn=cS.n, _tp=cT.n):
                    v.wait_ge(cS.h, _sn)
                    v.wait_ge(cT.h, _tp)
                    v.tensor_mul(sqk[:], ktT[:, T * pt:T * pt + T],
                                 ktT[:, T * pt:T * pt + T])
                    cV.inc(v.engine_nop())
                on("vector", fvk)
                for n in range(4):
                    vn = cV.n

                    def ftk(t, pt=pt, n=n, vn=vn, _vp=cV.n):
                        t.wait_ge(cV.h, _vp)
                        cT.inc(t.matmul(psKN[:], iall[:, 16 * pt:16 * pt + 16],
                                        sqk[:, 512 * n:512 * n + 512],
                                        start=True, stop=True))
                    on("tensor", ftk)
                    tn = cT.n

                    def fvkm(v, pt=pt, n=n, tn=tn):
                        v.wait_ge(cT.h, tn)
                        v.reduce_max(smallA[0:16, 8:9], psKN[:], AX)
                        VE(v)
                        if pt == 0 and n == 0:
                            v.tensor_copy(kmax2[:], smallA[0:16, 8:9])
                        else:
                            v.tensor_max(kmax2[:], kmax2[:], smallA[0:16, 8:9])
                        cV.inc(v.engine_nop())
                    on("vector", fvkm)

    def s2(b, ktT, vbuf):
        with contextlib.ExitStack() as ps:
            def psb(name, shape, dt=F32):
                return ps.enter_context(nc.sbuf_tensor(_un(name), shape, dt))
            qTc = psb("s2_qTc", [128, 8 * 512], BF16)
            fTc = psb("s2_fTc", [128, 8 * 512], BF16)
            upb = psb("s2_upb", [128, 8 * 512], BF16)
            kvro = psb("s2_kvro", [128, 512], BF16)
            Eb = psb("s2_Eb", [128, 512], BF16)
            expFA = psb("s2_expFA", [16, 4 * 512], BF16)
            sumFA = psb("s2_sumFA", [16, 512], F32)
            expS = psb("s2_expS", [128, 3 * 2048], BF16)
            attnC = psb("s2_attnC", [128, 8 * 512], F32)
            attnpre = psb("s2_attnpre", [128, 8 * 512], BF16)
            dccol = psb("s2_dccol", [128, 8 * 512], F32)
            dct = psb("s2_dct", [16, 512], F32)
            den16 = psb("s2_den16", [16, 512], F32)
            recip16 = psb("s2_recip16", [16, 512], F32)
            negc = psb("s2_negc", [16, 1], F32)
            negcb = psb("s2_negcb", [128, 16], F32)
            wring = psb("s2_wring", [128, 2 * 1024], BF16)
            u1t = psb("s2_u1t", [128, 1024], F32)
            fxt = psb("s2_fxt", [128, 512], F32)
            psW0 = ps.enter_context(nc.psum_tensor(_un("s2_psW0"), [128, 1024], F32))
            psW1 = ps.enter_context(nc.psum_tensor(_un("s2_psW1"), [128, 1024], F32))
            psE = ps.enter_context(nc.psum_tensor(_un("s2_psE"), [128, 512], F32))
            psO = ps.enter_context(nc.psum_tensor(_un("s2_psO"), [128, 512], F32))
            psD = ps.enter_context(nc.psum_tensor(_un("s2_psD"), [128, 512], F32))
            psX = ps.enter_context(nc.psum_tensor(_un("s2_psX"), [128, 512], F32))

            for c in range(NCH):
                nkt = 4 * c + 4
                vfree = cV.n
                tfree = cT.n

                def flq(sync, c=c, vfree=vfree, tfree=tfree):
                    sync.wait_ge(cV.h, vfree)
                    sync.wait_ge(cT.h, tfree)
                    dS.dma(sync.dma_start(
                        out=qTc[:].rearrange("p (o t) -> p o t", o=8),
                        in_=qT_s[:, :, 512 * c:512 * c + 512]
                        .rearrange("o p t -> p o t")))
                    dS.dma(sync.dma_start(
                        out=fTc[:].rearrange("p (o t) -> p o t", o=8),
                        in_=fT_s[:, :, 512 * c:512 * c + 512]
                        .rearrange("o p t -> p o t")))
                on("sync", flq)
                ldqf = dS.n

                for fi in range(NF):
                    pFA = psW0 if fi < 2 else psW1
                    sl = slice(512 * (fi % 2), 512 * (fi % 2) + 512)
                    for pt in range(8):
                        vn = cV.n
                        scn = cS.n

                        def ftu(t, fi=fi, pt=pt, vn=vn, scn=scn):
                            t.wait_ge(dS.h, ldqf)
                            t.wait_ge(cS.h, scn)
                            t.wait_ge(cV.h, vn)
                            _li = t.matmul(psX[:], bdw[:, 128 * fi:128 * fi + 128],
                                     fTc[:, 512 * pt:512 * pt + 512],
                                     start=True, stop=True)
                            cT.inc(_li)
                        on("tensor", ftu)
                        tn = cT.n

                        def fue(s, pt=pt, tn=tn):
                            s.wait_ge(cT.h, tn)
                            _li = s.copy(upb[:, 512 * pt:512 * pt + 512], psX[:])
                            cS.inc(_li)
                        on("scalar", fue)
                    for o in range(8):
                        vn = cV.n

                        def flw(sync, o=o, vn=vn):
                            sync.wait_ge(cV.h, vn)
                            dS.dma(sync.dma_start(
                                out=wring[:, 1024 * (o % 2):1024 * (o % 2) + 1024]
                                .rearrange("p (k c) -> p k c", k=8),
                                in_=wqkvfT_s[8 + o].rearrange("k p c -> p k c")))
                        on("sync", flw)
                        ldw = dS.n
                        vn = cV.n
                        scn = cS.n

                        def ftk(t, o=o, fi=fi, ldw=ldw, vn=vn, scn=scn, pFA=pFA, sl=sl):
                            t.wait_ge(dS.h, ldw)
                            t.wait_ge(cV.h, vn)
                            t.wait_ge(cS.h, scn)
                            for k in range(8):
                                _li = t.matmul(psX[:],
                                         wring[:, 1024 * (o % 2) + 128 * k:
                                               1024 * (o % 2) + 128 * k + 128],
                                         upb[:, 512 * k:512 * k + 512],
                                         start=(k == 0), stop=(k == 7))
                            cT.inc(_li)
                        on("tensor", ftk)
                        tn = cT.n

                        def fke(v, o=o, tn=tn):
                            v.wait_ge(cT.h, tn)
                            v.tensor_copy(kvro[:], psX[:])
                            VE(v)
                            v.tensor_mul(Eb[:], qTc[:, 512 * o:512 * o + 512], kvro[:])
                            cV.inc(v.engine_nop())
                        on("vector", fke)
                        vn = cV.n

                        def ftf(t, o=o, vn=vn, pFA=pFA, sl=sl):
                            t.wait_ge(cV.h, vn)
                            _li = t.matmul(pFA[0:16, sl], iall[:, 16 * o:16 * o + 16],
                                     Eb[:], start=(o == 0), stop=(o == 7))
                            cT.inc(_li)
                        on("tensor", ftf)

                prev_tq = cT.n
                for o in range(8):
                    vn = cV.n

                    def fvq(v, o=o, prev_tq=prev_tq):
                        v.wait_ge(dS.h, ldqf)
                        v.wait_ge(cT.h, prev_tq)
                        v.tensor_mul(Eb[:], qTc[:, 512 * o:512 * o + 512],
                                     qTc[:, 512 * o:512 * o + 512])
                        cV.inc(v.engine_nop())
                    on("vector", fvq)
                    vn = cV.n

                    def ftq(t, o=o, vn=vn):
                        t.wait_ge(cV.h, vn)
                        _li = t.matmul(psE[0:16, :], iall[:, 16 * o:16 * o + 16], Eb[:],
                                 start=(o == 0), stop=(o == 7))
                        cT.inc(_li)
                    on("tensor", ftq)
                    prev_tq = cT.n
                tn = cT.n

                def fvc(v, tn=tn):
                    v.wait_ge(cT.h, tn)
                    v.reduce_max(smallA[0:16, 0:1], psE[0:16, :], AX)
                    v.reduce_max(smallA[0:16, 2:3], psW0[0:16, 0:512], AX)
                    v.reduce_max(smallA[0:16, 3:4], psW0[0:16, 512:1024], AX)
                    v.reduce_max(smallA[0:16, 4:5], psW1[0:16, 0:512], AX)
                    v.reduce_max(smallA[0:16, 5:6], psW1[0:16, 512:1024], AX)
                    v.memset(tbufA[:], 0.0)
                    VE(v)
                    v.tensor_add(smallA[0:16, 1:2], smallA[0:16, 0:1], kmax2[:])
                    v.tensor_max(smallA[0:16, 2:3], smallA[0:16, 2:3], smallA[0:16, 3:4])
                    v.tensor_max(smallA[0:16, 4:5], smallA[0:16, 4:5], smallA[0:16, 5:6])
                    VE(v)
                    v.tensor_scalar_mul(smallA[0:16, 1:2], smallA[0:16, 1:2], 0.0625)
                    v.tensor_max(smallA[0:16, 2:3], smallA[0:16, 2:3], smallA[0:16, 4:5])
                    VE(v)
                    v.tensor_max(smallA[0:16, 1:2], smallA[0:16, 1:2], smallA[0:16, 2:3])
                    VE(v)
                    v.tensor_scalar_mul(negc[:], smallA[0:16, 1:2], -1.0)
                    VE(v)
                    v.tensor_copy(tbufA[0:16, 0:1], negc[:])
                    cV.inc(v.engine_nop())
                on("vector", fvc)
                vn = cV.n

                def ftc(t, vn=vn):
                    t.wait_ge(cV.h, vn)
                    _li = t.transpose(psX[:, 0:128], tbufA[:], identf[:])
                    cT.inc(_li)
                on("tensor", ftc)
                tn = cT.n

                def fvc2(v, tn=tn):
                    v.wait_ge(cT.h, tn)
                    v.tensor_copy(smallB[0:1, 0:16], psX[0:1, 0:16])
                    cV.inc(v.engine_nop())
                on("vector", fvc2)
                vn = cV.n

                def ftc2(t, vn=vn):
                    t.wait_ge(cV.h, vn)
                    cT.inc(t.matmul(psX[:, 0:16], onesf[:], smallB[0:1, 0:16],
                                    start=True, stop=True))
                on("tensor", ftc2)
                tn = cT.n

                def fvc3(v, tn=tn):
                    v.wait_ge(cT.h, tn)
                    v.tensor_copy(negcb[:], psX[:, 0:16])
                    cV.inc(v.engine_nop())
                on("vector", fvc3)
                vncb = cV.n

                def fse(s, vncb=vncb):
                    s.wait_ge(cV.h, vncb)
                    for fi in range(NF):
                        pFA = psW0 if fi < 2 else psW1
                        sl = slice(512 * (fi % 2), 512 * (fi % 2) + 512)
                        _li = s.activation(expFA[:, 512 * fi:512 * fi + 512],
                                     pFA[0:16, sl], ACT.Exp, bias=negc[:])
                    cS.inc(_li)
                on("scalar", fse)
                sn = cS.n

                def fvsf(v, sn=sn):
                    v.wait_ge(cS.h, sn)
                    v.tensor_add(den16[:], expFA[:, 0:512], expFA[:, 512:1024])
                    v.tensor_add(sumFA[:], expFA[:, 1024:1536], expFA[:, 1536:2048])
                    VE(v)
                    v.tensor_add(sumFA[:], sumFA[:], den16[:])
                    cV.inc(v.engine_nop())
                on("vector", fvsf)
                sumfa_n = cV.n

                for o in range(8):
                    h1r, h2r = PHID[2 * o], PHID[2 * o + 1]
                    for ktp in range(0, nkt, 2):
                        ring = (ktp // 2) % 3
                        exn = cS.n
                        vmn = cV.n
                        pvn = cT.n

                        def fts(t, o=o, ktp=ktp, exn=exn):
                            t.wait_ge(cS.h, exn)
                            for kd in range(2):
                                kt = ktp + kd
                                _li = t.matmul(psW0[:, 512 * kd:512 * kd + 512],
                                         ktT[0:64, T * o + 128 * kt:T * o + 128 * kt + 128],
                                         qTc[0:64, 512 * o:512 * o + 512],
                                         start=True, stop=True)
                                _li = t.matmul(psW1[:, 512 * kd:512 * kd + 512],
                                         ktT[64:128, T * o + 128 * kt:T * o + 128 * kt + 128],
                                         qTc[64:128, 512 * o:512 * o + 512],
                                         start=True, stop=True)
                            cT.inc(_li)
                        on("tensor", fts)
                        tn = cT.n

                        def fex(s, o=o, ring=ring, h1r=h1r, h2r=h2r, tn=tn, vmn=vmn):
                            s.wait_ge(cT.h, tn)
                            s.wait_ge(cV.h, vmn)
                            _li = s.activation(expS[:, 2048 * ring:2048 * ring + 1024],
                                         psW0[:], ACT.Exp, scale=0.125,
                                         bias=negcb[:, h1r:h1r + 1])
                            _li = s.activation(expS[:, 2048 * ring + 1024:2048 * ring + 2048],
                                         psW1[:], ACT.Exp, scale=0.125,
                                         bias=negcb[:, h2r:h2r + 1])
                            cS.inc(_li)
                        on("scalar", fex)
                        sn = cS.n
                        mask_ops = []
                        for kd in range(2):
                            kt = ktp + kd
                            roff = 128 * kt - 512 * c
                            if roff >= 0:
                                mask_ops.append((kd, roff))
                        if mask_ops:
                            def fmk(v, ring=ring, mask_ops=mask_ops, sn=sn):
                                v.wait_ge(cS.h, sn)
                                for kd, roff in mask_ops:
                                    for hh in range(2):
                                        sl2 = slice(2048 * ring + 1024 * hh + 512 * kd,
                                                    2048 * ring + 1024 * hh + 512 * kd + 512)
                                        v.tensor_mul(expS[:, sl2], expS[:, sl2],
                                                     bigmask[:, 384 - roff:896 - roff])
                                cV.inc(v.engine_nop())
                            on("vector", fmk)
                        mkn = cV.n

                        def fpv(t, o=o, ktp=ktp, ring=ring, mask_ops=mask_ops,
                                sn=sn, mkn=mkn, nkt=nkt):
                            if mask_ops:
                                t.wait_ge(cV.h, mkn)
                            else:
                                t.wait_ge(cS.h, sn)
                            for kd in range(2):
                                kt = ktp + kd
                                e1 = expS[:, 2048 * ring + 512 * kd:
                                          2048 * ring + 512 * kd + 512]
                                e2 = expS[:, 2048 * ring + 1024 + 512 * kd:
                                          2048 * ring + 1024 + 512 * kd + 512]
                                vb = vbuf[:, 1040 * kt + 65 * (2 * o):
                                          1040 * kt + 65 * (2 * o) + 65]
                                vb2 = vbuf[:, 1040 * kt + 65 * (2 * o + 1):
                                           1040 * kt + 65 * (2 * o + 1) + 64]
                                _li = t.matmul(psE[0:65, :], vb, e1,
                                         start=(kt == 0), stop=(kt == nkt - 1))
                                _li = t.matmul(psO[64:128, :], vb2, e2,
                                         start=(kt == 0), stop=(kt == nkt - 1),
                                         tile_position=(0, 64))
                                _li = t.matmul(psD[0:1, :], onescol_b[:], e2,
                                         start=(kt == 0), stop=(kt == nkt - 1))
                            cT.inc(_li)
                        on("tensor", fpv)
                    tn = cT.n

                    def fpe(s, o=o, tn=tn):
                        s.wait_ge(cT.h, tn)
                        _li = s.copy(attnC[0:64, 512 * o:512 * o + 512], psE[0:64, :])
                        _li = s.copy(attnC[64:128, 512 * o:512 * o + 512], psO[64:128, :])
                        cS.inc(_li)
                    on("scalar", fpe)

                    def fpd(v, o=o, tn=tn, _dp=dS.n):
                        v.wait_ge(cT.h, tn)
                        v.wait_ge(dS.h, _dp)
                        v.tensor_copy(dccol[64:65, 512 * o:512 * o + 512], psE[64:65, :])
                        v.tensor_copy(dccol[0:1, 512 * o:512 * o + 512], psD[0:1, :])
                        cV.inc(v.engine_nop())
                    on("vector", fpd)
                vdn = cV.n
                pairs_scalar = cS.n

                def fdd(sync, vdn=vdn):
                    sync.wait_ge(cV.h, vdn)
                    dS.dma(sync.dma_start(out=dct_s[0], in_=dccol[64:65, :]))
                    dS.dma(sync.dma_start(out=dct_s[1], in_=dccol[0:1, :]))
                on("sync", fdd)
                dd0 = dS.n

                def fdd2(sync, dd0=dd0):
                    sync.wait_ge(dS.h, dd0)
                    dS.dma(sync.dma_start(
                        out=dct[0:8, :],
                        in_=dct_s[0].rearrange("q (o t) -> (q o) t", o=8)))
                    dS.dma(sync.dma_start(
                        out=dct[8:16, :],
                        in_=dct_s[1].rearrange("q (o t) -> (q o) t", o=8)))
                on("sync", fdd2)
                ddn = dS.n

                def fdr(v, ddn=ddn):
                    v.wait_ge(dS.h, ddn)
                    v.wait_ge(cV.h, sumfa_n)
                    v.tensor_add(den16[:], dct[:], sumFA[:])
                    VE(v)
                    v.reciprocal(recip16[:], den16[:])
                    cV.inc(v.engine_nop())
                on("vector", fdr)
                rcn = cV.n

                for fi in range(NF):
                    for pt in range(8):
                        scn = cS.n

                        def ftu2(t, fi=fi, pt=pt, scn=scn):
                            t.wait_ge(cS.h, scn)
                            _li = t.matmul(psX[:], bdw[:, 128 * fi:128 * fi + 128],
                                     fTc[:, 512 * pt:512 * pt + 512],
                                     start=True, stop=True)
                            cT.inc(_li)
                        on("tensor", ftu2)
                        tn = cT.n

                        def fue2(s, pt=pt, tn=tn):
                            s.wait_ge(cT.h, tn)
                            _li = s.copy(upb[:, 512 * pt:512 * pt + 512], psX[:])
                            cS.inc(_li)
                        on("scalar", fue2)
                    for o in range(8):
                        vn = cV.n

                        def flw2(sync, o=o, vn=vn):
                            sync.wait_ge(cV.h, vn)
                            dS.dma(sync.dma_start(
                                out=wring[:, 1024 * (o % 2):1024 * (o % 2) + 1024]
                                .rearrange("p (k c) -> p k c", k=8),
                                in_=wqkvfT_s[16 + o].rearrange("k p c -> p k c")))
                        on("sync", flw2)
                        ldw = dS.n
                        vn = cV.n
                        scn = cS.n

                        def ftv2(t, o=o, ldw=ldw, vn=vn, scn=scn):
                            t.wait_ge(dS.h, ldw)
                            t.wait_ge(cV.h, vn)
                            t.wait_ge(cS.h, scn)
                            for k in range(8):
                                _li = t.matmul(psX[:],
                                         wring[:, 1024 * (o % 2) + 128 * k:
                                               1024 * (o % 2) + 128 * k + 128],
                                         upb[:, 512 * k:512 * k + 512],
                                         start=(k == 0), stop=(k == 7))
                            cT.inc(_li)
                        on("tensor", ftv2)
                        tn = cT.n

                        def fve2(v, tn=tn):
                            v.wait_ge(cT.h, tn)
                            v.tensor_copy(kvro[:], psX[:])
                            cV.inc(v.engine_nop())
                        on("vector", fve2)
                        vn2 = cV.n

                        def ftb2(t, o=o, fi=fi, vn2=vn2):
                            t.wait_ge(cV.h, vn2)
                            cT.inc(t.matmul(psW0[:, 0:512],
                                            itall[:, 128 * o:128 * o + 128],
                                            expFA[:, 512 * fi:512 * fi + 512],
                                            start=True, stop=True))
                        on("tensor", ftb2)
                        tn2 = cT.n

                        def ffx(v, o=o, tn2=tn2):
                            v.wait_ge(cT.h, tn2)
                            v.wait_ge(cS.h, pairs_scalar)
                            v.tensor_mul(fxt[:], psW0[:, 0:512], kvro[:])
                            VE(v)
                            v.tensor_add(attnC[:, 512 * o:512 * o + 512],
                                         attnC[:, 512 * o:512 * o + 512], fxt[:])
                            cV.inc(v.engine_nop())
                        on("vector", ffx)

                for o in range(8):
                    vprev = cV.n

                    def ftr2(t, o=o, rcn=rcn, vprev=vprev):
                        t.wait_ge(cV.h, max(rcn, vprev))
                        cT.inc(t.matmul(psX[:], selpair[:, 128 * o:128 * o + 128],
                                        recip16[:], start=True, stop=True))
                    on("tensor", ftr2)
                    tn = cT.n

                    def fap(v, o=o, tn=tn):
                        v.wait_ge(cT.h, tn)
                        v.tensor_mul(attnpre[:, 512 * o:512 * o + 512],
                                     attnC[:, 512 * o:512 * o + 512], psX[:])
                        cV.inc(v.engine_nop())
                    on("vector", fap)
                apn = cV.n

                wring_guard = [cT.n, cT.n]
                for grp in range(2):
                    for o in range(8):
                        def flp(sync, o=o, grp=grp, wg=wring_guard[o % 2]):
                            sync.wait_ge(cT.h, wg)
                            dS.dma(sync.dma_start(
                                out=wring[:, 1024 * (o % 2):1024 * (o % 2) + 1024],
                                in_=wprojT_s[o]))
                        on("sync", flp)
                        ldw = dS.n

                        def ftp(t, o=o, grp=grp, ldw=ldw, apn=apn):
                            t.wait_ge(dS.h, ldw)
                            t.wait_ge(cV.h, apn)
                            for mtl in range(2):
                                mt = 2 * grp + mtl
                                pdst = psW0 if mtl == 0 else psW1
                                lhsT = attnpre[:, 512 * o + 128 * mt:512 * o + 128 * mt + 128]
                                _li = t.matmul(pdst[:, 0:512], lhsT,
                                         wring[:, 1024 * (o % 2):1024 * (o % 2) + 512],
                                         start=(o == 0), stop=(o == 7))
                                _li = t.matmul(pdst[:, 512:1024], lhsT,
                                         wring[:, 1024 * (o % 2) + 512:1024 * (o % 2) + 1024],
                                         start=(o == 0), stop=(o == 7))
                            cT.inc(_li)
                        on("tensor", ftp)
                        wring_guard[o % 2] = cT.n
                    tn = cT.n
                    for mtl in range(2):
                        mt = 2 * grp + mtl

                        def flu(sync, mt=mt, c=c, _vp=cV.n, _dp=dS.n):
                            sync.wait_ge(cV.h, _vp)
                            sync.wait_ge(dS.h, _dp)
                            dS.dma(sync.dma_start(out=u1t[:], in_=u_s[4 * c + mt]))
                        on("sync", flu)
                        ldu = dS.n

                        def fu1(v, mtl=mtl, tn=tn, ldu=ldu):
                            v.wait_ge(cT.h, tn)
                            v.wait_ge(dS.h, ldu)
                            pdst = psW0 if mtl == 0 else psW1
                            v.tensor_add(u1t[:], pdst[:], u1t[:])
                            cV.inc(v.engine_nop())
                        on("vector", fu1)
                        vn = cV.n

                        def fu1s(sync, mt=mt, c=c, vn=vn):
                            sync.wait_ge(cV.h, vn)
                            dS.dma(sync.dma_start(out=u1_s[4 * c + mt], in_=u1t[:]))
                        on("sync", fu1s)

    def s3(b):
        with contextlib.ExitStack() as ps:
            def psb(name, shape, dt=F32):
                return ps.enter_context(nc.sbuf_tensor(_un(name), shape, dt))
            h2T = psb("s3_h2T", [128, 8 * 1024], BF16)
            gT = psb("s3_gT", [128, 32 * 1024], BF16)
            wf1r = psb("s3_wf1r", [128, 2 * 1024], BF16)
            wf2r = psb("s3_wf2r", [128, 2 * 512], BF16)
            xt2 = psb("s3_xt2", [128, 2 * 1024], F32)
            sqb = psb("s3_sqb", [128, 1024], F32)
            h2f = psb("s3_h2f", [128, 1024], F32)
            u2t = psb("s3_u2t", [128, 1024], F32)
            xat = psb("s3_xat", [128, 512], F32)
            ps3 = [ps.enter_context(nc.psum_tensor(_un(f"s3_ps{i}"), [128, 1024], F32))
                   for i in range(4)]

            for half, row_hbm in [(0, ln2g), (1, ln2b)]:
                def flr(sync, row_hbm=row_hbm, _tp=cT.n):
                    sync.wait_ge(cT.h, _tp)
                    dS.dma(sync.dma_start(out=lnrow[:], in_=row_hbm[None, :]))
                on("sync", flr)
                ldr = dS.n

                def ftg(t, half=half, ldr=ldr, _vp=cV.n):
                    t.wait_ge(dS.h, ldr)
                    t.wait_ge(cV.h, _vp)
                    pdst = ps3[0] if half == 0 else ps3[1]
                    for n in range(2):
                        cT.inc(t.matmul(
                            pdst[:, 512 * n:512 * n + 512], onesf[:],
                            lnrow[0:1, 512 * n:512 * n + 512],
                            start=True, stop=True))
                on("tensor", ftg)
            tn = cT.n

            def fvg(v, tn=tn):
                v.wait_ge(cT.h, tn)
                v.tensor_copy(gb_bc[:, 0:D], ps3[0][:])
                v.tensor_copy(gb_bc[:, D:2 * D], ps3[1][:])
                cV.inc(v.engine_nop())
            on("vector", fvg)
            gbn = cV.n

            for hf in range(2):
                ev_done = {}
                for i in range(8):
                    ti = 8 * hf + i
                    prevev = ev_done.get(i - 2, 0)

                    def fs(sync, ti=ti, i=i, prevev=prevev):
                        sync.wait_ge(cV.h, prevev)
                        dS.dma(sync.dma_start(out=xt2[:, D * (i % 2):D * (i % 2) + D],
                                              in_=u1_s[ti]))
                    on("sync", fs)
                    ldn = dS.n
                    xs = xt2[:, D * (i % 2):D * (i % 2) + D]

                    def fv1(v, xs=xs, ldn=ldn, i=i, _sn=cS.n):
                        v.wait_ge(dS.h, ldn)
                        v.wait_ge(cS.h, _sn)
                        v.reduce_sum(stats[:, 0:1], xs, AX)
                        v.tensor_mul(sqb[:], xs, xs)
                        v.reduce_sum(stats[:, 1:2], sqb[:], AX)
                        v.tensor_scalar_mul(stats[:, 2:3], stats[:, 0:1], float(1.0 / D))
                        v.tensor_scalar_mul(stats[:, 3:4], stats[:, 1:2], float(1.0 / D))
                        VE(v)
                        v.tensor_mul(stats[:, 4:5], stats[:, 2:3], stats[:, 2:3])
                        VE(v)
                        v.tensor_sub(stats[:, 3:4], stats[:, 3:4], stats[:, 4:5])
                        cV.inc(v.engine_nop())
                    on("vector", fv1)
                    vn = cV.n

                    def fs1(s, vn=vn):
                        s.wait_ge(cV.h, vn)
                        _li = s.activation(stats[:, 4:5], stats[:, 3:4], ACT.Ln, bias=float(EPS))
                        _li = s.activation(stats[:, 5:6], stats[:, 4:5], ACT.Exp, scale=-0.5)
                        cS.inc(_li)
                    on("scalar", fs1)
                    sn = cS.n

                    def fv2(v, sn=sn):
                        v.wait_ge(cS.h, sn)
                        v.scalar_tensor_tensor(stats[:, 6:7], stats[:, 2:3], -1.0,
                                               stats[:, 5:6], ALU.mult, ALU.mult)
                        cV.inc(v.engine_nop())
                    on("vector", fv2)
                    vn = cV.n

                    def fv3(v, xs=xs, vn=vn, gbn=gbn, _tp=cT.n):
                        v.wait_ge(cV.h, max(vn, gbn))
                        v.wait_ge(cT.h, _tp)
                        v.tensor_scalar(h2f[:], xs, stats[:, 5:6], stats[:, 6:7],
                                        ALU.mult, ALU.add)
                        v.tensor_mul(h2f[:], h2f[:], gb_bc[:, 0:D])
                        v.tensor_add(h2f[:], h2f[:], gb_bc[:, D:2 * D])
                        cV.inc(v.engine_nop())
                    on("vector", fv3)
                    vn = cV.n
                    prev_ev1 = ev_done.get(i - 1, 0)

                    def ftr(t, vn=vn, prev_ev1=prev_ev1):
                        t.wait_ge(cV.h, max(vn, prev_ev1))
                        for j in range(8):
                            _li = t.transpose(ps3[2][:, 128 * j:128 * j + 128],
                                        h2f[:, 128 * j:128 * j + 128], identf[:])
                        cT.inc(_li)
                    on("tensor", ftr)
                    tn = cT.n

                    def fev(v, i=i, tn=tn):
                        v.wait_ge(cT.h, tn)
                        v.tensor_copy(
                            h2T[:].rearrange("p (j t) -> p j t", j=8)[:, :, 128 * i:128 * i + 128],
                            ps3[2][:].rearrange("p (j t) -> p j t", j=8))
                        cV.inc(v.engine_nop())
                        ev_done[i] = cV.n
                    on("vector", fev)
                h2T_done = cV.n

                for o in range(32):
                    scn = cS.n
                    cgn = cG.n

                    def flw(g, o=o, scn=scn):
                        g.wait_ge(cS.h, scn)
                        dG.dma(g.dma_start(
                            out=wf1r[:, 1024 * (o % 2):1024 * (o % 2) + 1024]
                            .rearrange("p (k n) -> p k n", k=8),
                            in_=wff1[:, 128 * o:128 * o + 128]
                            .rearrange("(k p) n -> p k n", p=128)))
                    on("gpsimd", flw)
                    ldw = dG.n
                    scn = cS.n

                    def ft1(t, o=o, ldw=ldw, scn=scn, h2T_done=h2T_done):
                        t.wait_ge(dG.h, ldw)
                        t.wait_ge(cV.h, h2T_done)
                        t.wait_ge(cS.h, scn)
                        pdst = ps3[o % 2]
                        for k in range(8):
                            lhsT = wf1r[:, 1024 * (o % 2) + 128 * k:
                                        1024 * (o % 2) + 128 * k + 128]
                            for n in range(2):
                                _li = t.matmul(pdst[:, 512 * n:512 * n + 512], lhsT,
                                         h2T[:, 1024 * k + 512 * n:1024 * k + 512 * n + 512],
                                         start=(k == 0), stop=(k == 7))
                        cT.inc(_li)
                    on("tensor", ft1)
                    tn = cT.n

                    def fg1(s, o=o, tn=tn):
                        s.wait_ge(cT.h, tn)
                        _li = s.activation(gT[:, 1024 * o:1024 * o + 1024],
                                     ps3[o % 2][:], ACT.Gelu)
                        cS.inc(_li)
                    on("scalar", fg1)
                gdone = cS.n

                u2n = {}
                wf2_guard = [cT.n, cT.n]
                for n in range(2):
                    for cc in range(32):
                        def flw2(g, cc=cc, n=n, wg=wf2_guard[cc % 2]):
                            g.wait_ge(cT.h, wg)
                            dG.dma(g.dma_start(
                                out=wf2r[:, 512 * (cc % 2):512 * (cc % 2) + 512],
                                in_=wff2[128 * cc:128 * cc + 128, 512 * n:512 * n + 512]))
                        on("gpsimd", flw2)
                        ldw = dG.n

                        def ft2(t, cc=cc, n=n, ldw=ldw, gdone=gdone, _vp=cV.n):
                            t.wait_ge(dG.h, ldw)
                            t.wait_ge(cS.h, gdone)
                            if cc == 0:
                                t.wait_ge(cV.h, _vp)
                            for tile in range(8):
                                _li = t.matmul(ps3[tile // 2][:, 512 * (tile % 2):512 * (tile % 2) + 512],
                                         gT[:, 1024 * cc + 128 * tile:1024 * cc + 128 * tile + 128],
                                         wf2r[:, 512 * (cc % 2):512 * (cc % 2) + 512],
                                         start=(cc == 0), stop=(cc == 31))
                            cT.inc(_li)
                        on("tensor", ft2)
                        wf2_guard[cc % 2] = cT.n
                    tn = cT.n
                    for tile in range(8):
                        ti = 8 * hf + tile

                        def flu1(sync, ti=ti, n=n, _vp=cV.n):
                            sync.wait_ge(cV.h, _vp)
                            dS.dma(sync.dma_start(
                                out=u2t[:, 0:512],
                                in_=u1_s[ti][:, 512 * n:512 * n + 512]))
                            dS.dma(sync.dma_start(
                                out=u2t[:, 512:1024],
                                in_=x_in[b, 128 * ti:128 * ti + 128,
                                         512 * n:512 * n + 512]))
                        on("sync", flu1)
                        ldn2 = dS.n

                        def fu2b(v, tile=tile, n=n, tn=tn, ldn2=ldn2):
                            v.wait_ge(cT.h, tn)
                            v.wait_ge(dS.h, ldn2)
                            v.tensor_add(u2t[:, 0:512],
                                         ps3[tile // 2][:, 512 * (tile % 2):512 * (tile % 2) + 512],
                                         u2t[:, 0:512])
                            VE(v)
                            v.tensor_add(xat[:, 0:512], u2t[:, 0:512],
                                         u2t[:, 512:1024])
                            cV.inc(v.engine_nop())
                        on("vector", fu2b)
                        vn2 = cV.n

                        def fxs2(sync, ti=ti, n=n, vn2=vn2):
                            sync.wait_ge(cV.h, vn2)
                            dS.dma(sync.dma_start(
                                out=xa_s[ti][:, 512 * n:512 * n + 512],
                                in_=xat[:, 0:512]))
                        on("sync", fxs2)
                    u2n[n] = cV.n

    def s4(b):
        with contextlib.ExitStack() as ps:
            def psb(name, shape, dt=F32):
                return ps.enter_context(nc.sbuf_tensor(_un(name), shape, dt))
            ntT = psb("s4_ntT", [128, 8 * T], BF16)
            xat = psb("s4_xat", [128, 2 * 1024], F32)
            sqb = psb("s4_sqb", [128, 1024], F32)
            vq = psb("s4_vq", [128, 1024], F32)
            nq32 = psb("s4_nq32", [128, 1024], I32)
            xqb = psb("s4_xqb", [128, 1024], BF16)
            m1r = psb("s4_m1r", [128, 2 * 1024], BF16)
            hb = psb("s4_hb", [128, 2048], F32)
            hs2 = psb("s4_hs2", [128, 2048], F32)
            maxacc = psb("s4_maxacc", [128, 2048], F32)
            minacc = psb("s4_minacc", [128, 2048], F32)
            s2bc = psb("s4_s2bc", [128, 2048], F32)
            n2i = psb("s4_n2i", [128, 2048], I32)
            n2b = psb("s4_n2b", [128, 2048], BF16)
            psA = ps.enter_context(nc.psum_tensor(_un("s4_psA"), [128, 1024], F32))
            psB = ps.enter_context(nc.psum_tensor(_un("s4_psB"), [128, 1024], F32))
            psQ = ps.enter_context(nc.psum_tensor(_un("s4_psQ"), [128, 1024], BF16))
            psR = ps.enter_context(nc.psum_tensor(_un("s4_psR"), [128, 512], F32))

            ev_done = {}
            for i in range(NT):
                prevev = ev_done.get(i - 2, 0)

                def fs(sync, i=i, prevev=prevev):
                    sync.wait_ge(cV.h, prevev)
                    dS.dma(sync.dma_start(out=xat[:, D * (i % 2):D * (i % 2) + D],
                                          in_=xa_s[i]))
                on("sync", fs)
                ldn = dS.n
                xs = xat[:, D * (i % 2):D * (i % 2) + D]

                def fv1(v, xs=xs, ldn=ldn, _sn=cS.n):
                    v.wait_ge(dS.h, ldn)
                    v.wait_ge(cS.h, _sn)
                    v.reduce_sum(stats[:, 0:1], xs, AX)
                    v.tensor_mul(sqb[:], xs, xs)
                    v.reduce_sum(stats[:, 1:2], sqb[:], AX)
                    v.tensor_scalar_mul(stats[:, 2:3], stats[:, 0:1], float(1.0 / D))
                    v.tensor_scalar_mul(stats[:, 3:4], stats[:, 1:2], float(1.0 / D))
                    VE(v)
                    v.tensor_mul(stats[:, 4:5], stats[:, 2:3], stats[:, 2:3])
                    VE(v)
                    v.tensor_sub(stats[:, 3:4], stats[:, 3:4], stats[:, 4:5])
                    cV.inc(v.engine_nop())
                on("vector", fv1)
                vn = cV.n

                def fs1(s, vn=vn):
                    s.wait_ge(cV.h, vn)
                    _li = s.activation(stats[:, 4:5], stats[:, 3:4], ACT.Ln, bias=float(EPS))
                    _li = s.activation(stats[:, 5:6], stats[:, 4:5], ACT.Exp, scale=-0.5)
                    cS.inc(_li)
                on("scalar", fs1)
                sn = cS.n

                def fv2(v, sn=sn, i=i, xs=xs):
                    v.wait_ge(cS.h, sn)
                    v.scalar_tensor_tensor(stats[:, 6:7], stats[:, 2:3], -1.0,
                                           stats[:, 5:6], ALU.mult, ALU.mult)
                    cV.inc(v.engine_nop())
                on("vector", fv2)
                vn = cV.n

                def fv3(v, i=i, xs=xs, vn=vn):
                    v.wait_ge(cV.h, vn)
                    v.tensor_scalar(vq[:], xs, stats[:, 5:6], stats[:, 6:7],
                                    ALU.mult, ALU.add)
                    v.reduce_max(stats[:, 7:8], vq[:], AX, apply_absolute_value=True)
                    VE(v)
                    v.tensor_scalar_max(stats[:, 7:8], stats[:, 7:8], 1e-5)
                    VE(v)
                    v.tensor_mul(gam1[:, i:i + 1], stats[:, 7:8], meanw[:, 0:1])
                    v.reciprocal(stats[:, 6:7], stats[:, 7:8])
                    VE(v)
                    v.tensor_scalar_mul(gam1[:, i:i + 1], gam1[:, i:i + 1],
                                        float(1.0 / 127.0))
                    v.tensor_scalar_mul(stats[:, 7:8], stats[:, 6:7], 127.0)
                    cV.inc(v.engine_nop())
                on("vector", fv3)
                vn = cV.n

                def fv4(v, i=i, vn=vn, _tp=cT.n):
                    v.wait_ge(cV.h, vn)
                    v.wait_ge(cT.h, _tp)
                    v.tensor_scalar_mul(sqb[:], vq[:], stats[:, 7:8])
                    v.tensor_copy(nq32[:], sqb[:])
                    v.tensor_scalar_mul(xqb[:], nq32[:], gam1[:, i:i + 1])
                    cV.inc(v.engine_nop())
                on("vector", fv4)
                vn = cV.n
                prev1 = ev_done.get(i - 1, 0)

                def ftr(t, vn=vn, prev1=prev1):
                    t.wait_ge(cV.h, max(vn, prev1))
                    for j in range(8):
                        _li = t.transpose(psQ[:, 128 * j:128 * j + 128],
                                    xqb[:, 128 * j:128 * j + 128], ident16[:])
                    cT.inc(_li)
                on("tensor", ftr)
                tn = cT.n

                def fev(v, i=i, tn=tn):
                    v.wait_ge(cT.h, tn)
                    v.tensor_copy(
                        ntT[:].rearrange("p (j t) -> p j t", j=8)[:, :, 128 * i:128 * i + 128],
                        psQ[:].rearrange("p (j t) -> p j t", j=8))
                    cV.inc(v.engine_nop())
                    ev_done[i] = cV.n
                on("vector", fev)
            nt_done = cV.n

            def fmz(v):
                v.memset(maxacc[:], 0.0)
                v.memset(minacc[:], 0.0)
                cV.inc(v.engine_nop())
            on("vector", fmz)
            for ct in range(32):
                scn = cS.n

                def flm(sc, ct=ct, scn=scn):
                    sc.wait_ge(cS.h, scn)
                    dC.dma(sc.dma_start(
                        out=m1r[:, 1024 * (ct % 2):1024 * (ct % 2) + 1024]
                        .rearrange("p (k c) -> p k c", k=8),
                        in_=m1T_s[ct].rearrange("k p c -> p k c")))
                on("scalar", flm)
                ldm = dC.n
                scn = cS.n

                def ft1(t, ct=ct, ldm=ldm, scn=scn, nt_done=nt_done):
                    t.wait_ge(dC.h, ldm)
                    t.wait_ge(cV.h, nt_done)
                    t.wait_ge(cS.h, scn)
                    for k in range(8):
                        lhsT = m1r[:, 1024 * (ct % 2) + 128 * k:
                                   1024 * (ct % 2) + 128 * k + 128]
                        for n in range(4):
                            pdst = psA if n < 2 else psB
                            _li = t.matmul(pdst[:, 512 * (n % 2):512 * (n % 2) + 512], lhsT,
                                     ntT[:, T * k + 512 * n:T * k + 512 * n + 512],
                                     start=(k == 0), stop=(k == 7))
                    cT.inc(_li)
                on("tensor", ft1)
                tn = cT.n
                vprev = cV.n

                def fhe(s, tn=tn, _vp=cV.n, _gp=cG.n):
                    s.wait_ge(cT.h, tn)
                    s.wait_ge(cV.h, _vp)
                    s.wait_ge(cG.h, _gp)
                    _li = s.copy(hb[:, 0:1024], psA[:])
                    _li = s.copy(hb[:, 1024:2048], psB[:])
                    cS.inc(_li)
                on("scalar", fhe)
                sn = cS.n

                def fcv(v, ct=ct, sn=sn, _dp=dS.n):
                    v.wait_ge(cS.h, sn)
                    v.wait_ge(dS.h, _dp)
                    v.tensor_scalar_mul(hs2[:], hb[:], convw_pp[:, 3 * ct + 1:3 * ct + 2])
                    v.scalar_tensor_tensor(hs2[:, 1:2048], hb[:, 0:2047],
                                           convw_pp[:, 3 * ct:3 * ct + 1],
                                           hs2[:, 1:2048], ALU.mult, ALU.add)
                    cV.inc(v.engine_nop())
                on("vector", fcv)
                vn = cV.n

                def fcg(v, ct=ct, vn=vn):
                    v.scalar_tensor_tensor(hs2[:, 0:2047], hb[:, 1:2048],
                                           convw_pp[:, 3 * ct + 2:3 * ct + 3],
                                           hs2[:, 0:2047], ALU.mult, ALU.add)
                    cV.inc(v.engine_nop())
                on("vector", fcg)
                gn = cV.n

                def fsl(s, ct=ct, gn=gn):
                    s.wait_ge(cV.h, gn)
                    _li = s.activation(hs2[:], hs2[:], ACT.Silu,
                                 bias=convb_pp[:, ct:ct + 1])
                    cS.inc(_li)
                on("scalar", fsl)
                sn = cS.n

                def fmx(v, sn=sn):
                    v.wait_ge(cS.h, sn)
                    v.tensor_tensor(maxacc[:], maxacc[:], hs2[:], ALU.max)
                    v.tensor_tensor(minacc[:], minacc[:], hs2[:], ALU.min)
                    cV.inc(v.engine_nop())
                on("vector", fmx)
                vn = cV.n

                def fhs(sync, ct=ct, vn=vn):
                    sync.wait_ge(cV.h, vn)
                    dS.dma(sync.dma_start(out=h2w_s[ct], in_=hs2[:]))
                on("sync", fhs)
            mx_done = cV.n
            h2w_done = dS.n

            for j in range(16):
                def ftm(t, j=j, mx_done=mx_done, _vp=cV.n):
                    t.wait_ge(cV.h, max(mx_done, _vp))
                    _li = t.transpose(psR[:, 0:128], maxacc[:, 128 * j:128 * j + 128],
                                identf[:])
                    cT.inc(_li)
                on("tensor", ftm)
                tn = cT.n

                def fvm(v, j=j, tn=tn):
                    v.wait_ge(cT.h, tn)
                    v.reduce_max(smallA[:, 0:1], psR[:, 0:128], AX,
                                 apply_absolute_value=True)
                    VE(v)
                    v.tensor_scalar_max(smallA[:, 0:1], smallA[:, 0:1], 1e-5)
                    VE(v)
                    v.tensor_mul(gam2[:, j:j + 1], smallA[:, 0:1], meanw[:, 1:2])
                    v.reciprocal(smallA[:, 1:2], smallA[:, 0:1])
                    VE(v)
                    v.tensor_scalar_mul(gam2[:, j:j + 1], gam2[:, j:j + 1],
                                        float(1.0 / 127.0))
                    v.tensor_scalar_mul(smallB[:, j:j + 1], smallA[:, 1:2], 127.0)
                    cV.inc(v.engine_nop())
                on("vector", fvm)
                vn = cV.n

                def fts2(t, j=j, vn=vn):
                    t.wait_ge(cV.h, vn)
                    cT.inc(t.matmul(psR[0:1, 0:128],
                                    smallB[:, j:j + 1], identf[:],
                                    start=True, stop=True))
                on("tensor", fts2)
                tn = cT.n

                def fvr(v, j=j, tn=tn):
                    v.wait_ge(cT.h, tn)
                    v.tensor_copy(gb_bc[0:1, 128 * j:128 * j + 128], psR[0:1, 0:128])
                    cV.inc(v.engine_nop())
                on("vector", fvr)
            vn = cV.n
            for n4 in range(4):
                def ftb(t, n4=n4, vn=vn):
                    t.wait_ge(cV.h, vn)
                    cT.inc(t.matmul(psA[:, 512 * (n4 % 2):512 * (n4 % 2) + 512]
                                    if n4 < 2 else psB[:, 512 * (n4 % 2):512 * (n4 % 2) + 512],
                                    onesf[:], gb_bc[0:1, 512 * n4:512 * n4 + 512],
                                    start=True, stop=True))
                on("tensor", ftb)
            tn = cT.n

            def fvb(v, tn=tn):
                v.wait_ge(cT.h, tn)
                v.tensor_copy(s2bc[:, 0:1024], psA[:])
                v.tensor_copy(s2bc[:, 1024:2048], psB[:])
                cV.inc(v.engine_nop())
            on("vector", fvb)
            s2bc_done = cV.n

            for ct in range(32):
                vprev = cV.n

                def fhl(sync, ct=ct, vprev=vprev, _hd=h2w_done):
                    sync.wait_ge(cV.h, vprev)
                    sync.wait_ge(dS.h, _hd)
                    dS.dma(sync.dma_start(out=hs2[:], in_=h2w_s[ct]))
                on("sync", fhl)
                ldn = dS.n

                def fn2(v, ct=ct, ldn=ldn, s2bc_done=s2bc_done):
                    v.wait_ge(dS.h, ldn)
                    v.wait_ge(cV.h, s2bc_done)
                    v.tensor_mul(n2i[:], hs2[:], s2bc[:])
                    v.tensor_copy(n2b[:], n2i[:])
                    cV.inc(v.engine_nop())
                on("vector", fn2)
                vn = cV.n

                def fn2s(sync, ct=ct, vn=vn):
                    sync.wait_ge(cV.h, vn)
                    dS.dma(sync.dma_start(out=n2_s[ct], in_=n2b[:]))
                on("sync", fn2s)

        with contextlib.ExitStack() as ps:
            def psb(name, shape, dt=F32):
                return ps.enter_context(nc.sbuf_tensor(_un(name), shape, dt))
            m2res = psb("s4_m2res", [128, 32 * 1024], BF16)
            n2g = psb("s4_n2g", [128, 32 * 512], BF16)
            xar = psb("s4_xar", [128, 2 * 1024], F32)
            otb = psb("s4_otb", [128, 2 * 1024], F32)
            ps4 = [ps.enter_context(nc.psum_tensor(_un(f"s4b_ps{i}"), [128, 1024], F32))
                   for i in range(4)]

            def flm2(sc):
                dC.dma(sc.dma_start(
                    out=m2res[:].rearrange("p (c o) -> p c o", c=32),
                    in_=m2T_s.rearrange("c p o -> p c o")))
            on("scalar", flm2)
            ldm2 = dC.n

            for tg in range(4):
                vprev = cV.n

                def fln(sync, tg=tg, vprev=vprev):
                    sync.wait_ge(cV.h, vprev)
                    dS.dma(sync.dma_start(
                        out=n2g[:].rearrange("p (c t) -> p c t", c=32),
                        in_=n2_s[:, :, 512 * tg:512 * tg + 512]
                        .rearrange("c p t -> p c t")))
                    for tl in range(2):
                        dS.dma(sync.dma_start(
                            out=xar[:, 1024 * tl:1024 * tl + 1024],
                            in_=xa_s[4 * tg + tl]))
                on("sync", fln)
                ldn = dS.n
                vprev = cV.n

                def ftb2(t, tg=tg, ldn=ldn, ldm2=ldm2, vprev=vprev):
                    t.wait_ge(dS.h, ldn)
                    t.wait_ge(dC.h, ldm2)
                    t.wait_ge(cV.h, vprev)
                    for cc in range(32):
                        for tile in range(4):
                            lhsT = n2g[:, 512 * cc + 128 * tile:512 * cc + 128 * tile + 128]
                            for n in range(2):
                                _li = t.matmul(ps4[tile][:, 512 * n:512 * n + 512], lhsT,
                                         m2res[:, 1024 * cc + 512 * n:1024 * cc + 512 * n + 512],
                                         start=(cc == 0), stop=(cc == 31))
                    cT.inc(_li)
                on("tensor", ftb2)
                tn = cT.n

                for tile in range(4):
                    tt = 4 * tg + tile

                    def fxl(sync, tt=tt, tile=tile, tn=tn):
                        if tile >= 2:
                            sync.wait_ge(cV.h, cV.n)
                            dS.dma(sync.dma_start(
                                out=xar[:, 1024 * (tile % 2):1024 * (tile % 2) + 1024],
                                in_=xa_s[tt]))
                    on("sync", fxl)
                    ldx = dS.n

                    def fo(v, tile=tile, tt=tt, tn=tn, ldx=ldx):
                        v.wait_ge(cT.h, tn)
                        v.wait_ge(dS.h, ldx)
                        for n in range(2):
                            v.scalar_tensor_tensor(
                                otb[:, 1024 * (tile % 2) + 512 * n:
                                    1024 * (tile % 2) + 512 * n + 512],
                                ps4[tile][:, 512 * n:512 * n + 512],
                                gam2[:, tt:tt + 1],
                                xar[:, 1024 * (tile % 2) + 512 * n:
                                    1024 * (tile % 2) + 512 * n + 512],
                                ALU.mult, ALU.add)
                        cV.inc(v.engine_nop())
                    on("vector", fo)
                    vn = cV.n

                    def fos(sync, tt=tt, tile=tile, b=b, vn=vn):
                        sync.wait_ge(cV.h, vn)
                        dS.dma(sync.dma_start(
                            out=out_d[b, 128 * tt:128 * tt + 128, :],
                            in_=otb[:, 1024 * (tile % 2):1024 * (tile % 2) + 1024]))
                    on("sync", fos)

    import os
    _lim = os.environ.get("BASSK_PHASES", "all")
    _order = ["sc", "s0", "s1", "s2", "s3", "s4"]
    _upto = len(_order) if _lim == "all" else _order.index(_lim) + 1

    s0_consts()
    if _upto >= 2:
        s0_wprep()
    barrier()

    with contextlib.ExitStack() as bigs:
        ktT = bigs.enter_context(nc.sbuf_tensor("ktT", [128, 8 * T], BF16))
        vbuf = bigs.enter_context(nc.sbuf_tensor("vbuf", [128, NT * H * 65], BF16))
        for b in range(BL):
            if _upto >= 3:
                s1(b, ktT, vbuf)
                barrier()
            if _upto >= 4:
                s2(b, ktT, vbuf)
                barrier()
            if _upto >= 5:
                s3(b)
                barrier()
            if _upto >= 6:
                s4(b)
                barrier()
        _sbdump = (ktT, vbuf)

    _dump = os.environ.get("BASSK_DUMP", "")
    if _dump:
        _dmap = {"u_s": u_s, "qT_s": qT_s, "fT_s": fT_s, "u1_s": u1_s,
                 "xa_s": xa_s, "h2w_s": h2w_s, "n2_s": n2_s,
                 "wqkvfT_s": wqkvfT_s, "m1T_s": m1T_s, "m2T_s": m2T_s,
                 "wprojT_s": wprojT_s, "dct_s": dct_s,
                 "ktT": _sbdump[0], "vbuf": _sbdump[1]}
        _t = _dmap[_dump]
        _n = 1
        for _d in _t.shape[:-1]:
            _n *= _d
        _w = _t.shape[-1]
        _rows = min(_n, (BL * T * D) // _w)
        _flat_in = _t.rearrange(" ".join(f"d{i}" for i in range(len(_t.shape)))
                                + " -> (" + " ".join(f"d{i}" for i in range(len(_t.shape) - 1))
                                + f") d{len(_t.shape) - 1}")
        _flat_out = out_d.rearrange("b t d -> (b t d)")[0:_rows * _w].rearrange(
            "(r w) -> r w", w=_w)

        def fdmp(g):
            for ctr in [cT, cV, cS, cG]:
                g.wait_ge(ctr.h, ctr.n)
            g.wait_ge(dS.h, dS.n)
            dG.dma(g.dma_start(out=_flat_out[0:_rows, :], in_=_flat_in[0:_rows, :]))
        on("gpsimd", fdmp)

    def fin(sync):
        sync.wait_ge(dS.h, dS.n)
        sync.wait_ge(dG.h, dG.n)
        sync.wait_ge(dC.h, dC.n)
        sync.wait_ge(dT.h, dT.n)
    on("sync", fin)

    es.close()
    return nc


_NC_CACHE = []


def kernel(**inputs):
    x = np.ascontiguousarray(inputs["L0_attn_x"])
    weights = {
        "wqkvf": np.ascontiguousarray(inputs["L0_attn_W_qkvf"]),
        "wconv": np.ascontiguousarray(inputs["L0_attn_W_conv"]),
        "wproj": np.ascontiguousarray(inputs["L0_attn_W_proj"]),
        "wff1": np.ascontiguousarray(inputs["L0_attn_W_ff1"]),
        "wff2": np.ascontiguousarray(inputs["L0_attn_W_ff2"]),
        "ln1g": np.ascontiguousarray(inputs["L0_attn_ln1_g"]),
        "ln1b": np.ascontiguousarray(inputs["L0_attn_ln1_b"]),
        "ln2g": np.ascontiguousarray(inputs["L0_attn_ln2_g"]),
        "ln2b": np.ascontiguousarray(inputs["L0_attn_ln2_b"]),
        "w1": np.ascontiguousarray(inputs["L0_ffn_w1"]),
        "convw": np.ascontiguousarray(inputs["L0_ffn_conv_w"]),
        "convb": np.ascontiguousarray(inputs["L0_ffn_conv_b"]),
        "w2": np.ascontiguousarray(inputs["L0_ffn_w2"]),
    }
    if not _NC_CACHE:
        _NC_CACHE.append(build_nc())
    nc = _NC_CACHE[0]
    in_maps = []
    for core in range(8):
        m = {"x": np.ascontiguousarray(x[BL * core:BL * core + BL])}
        m.update(weights)
        in_maps.append(m)
    res = run_bass_kernel_spmd(nc, in_maps, core_ids=list(range(8)))
    return np.concatenate([r["out"] for r in res.results], axis=0)


# version 07 (final)
# speedup vs baseline: 1.2999x; 1.2999x over previous
import contextlib
import numpy as np
import ml_dtypes
import concourse.bass as bass
import concourse.mybir as mybir
from concourse.bass_utils import run_bass_kernel_spmd

F32 = mybir.dt.float32
BF16 = mybir.dt.bfloat16
I32 = mybir.dt.int32
AX = mybir.AxisListType.X
ALU = mybir.AluOpType
ACT = mybir.ActivationFunctionType
BF = ml_dtypes.bfloat16

T, D, DH, H, HS, NF = 2048, 1024, 4096, 16, 64, 4
NT, NCH, BL = 16, 4, 2
EPS = 1e-5

PHID = [(h % 2) * 8 + h // 2 for h in range(H)]


_SIG = {}


class Ctr:
    def __init__(self, es, nc, name):
        self.h = es.enter_context(nc.semaphore(name))
        self._n = 0
        self.last = None
        self.wmap = {}

    @property
    def n(self):
        if self.last is not None and self.last not in _SIG:
            last, self.last = self.last, None
            self.inc(last)
            self.last = last
        return self._n

    def inc(self, inst, k=1):
        inst.then_inc(self.h, k)
        self._n += k
        _SIG[inst] = (self, self._n)
        return inst

    def dma(self, inst):
        return self.inc(inst, 16)


class SerEng:
    __slots__ = ("e", "c", "dctrs", "coarse", "fenced")
    _PASS = frozenset(["wait_ge", "then_inc"])

    def __init__(self, e, c, dctrs=(), coarse=False):
        object.__setattr__(self, "e", e)
        object.__setattr__(self, "c", c)
        object.__setattr__(self, "dctrs", dctrs)
        object.__setattr__(self, "coarse", coarse)
        object.__setattr__(self, "fenced", False)

    def dma_start(self, *args, **kw):
        e = self.e
        c = self.c
        for d in self.dctrs:
            v = d.n
            key = ("dma", d)
            wm = (c.wmap if c is not None else d.wmap)
            if v > wm.get(key, 0):
                e.wait_ge(d.h, v)
                wm[key] = v
        return e.dma_start(*args, **kw)

    def __getattr__(self, name):
        a = getattr(self.e, name)
        if name in SerEng._PASS or not callable(a) or self.c is None:
            return a
        c = self.c
        e = self.e

        def w(*args, **kw):
            prev = c.last
            if self.coarse and self.fenced:
                inst = a(*args, **kw)
                c.last = inst
                return inst
            object.__setattr__(self, "fenced", True)
            if prev is not None:
                sig = _SIG.get(prev)
                if sig is None:
                    c.inc(prev)
                    sig = (c, c.n)
                ctr2, val = sig
                if val > c.wmap.get(ctr2, 0):
                    e.wait_ge(ctr2.h, val)
                    c.wmap[ctr2] = val
            inst = a(*args, **kw)
            c.last = inst
            return inst
        return w


def _consts():
    pp, cc = np.meshgrid(np.arange(128), np.arange(896), indexing="ij")
    I_all = np.zeros((128, 128), dtype=BF)
    IT_all = np.zeros((16, 1024), dtype=BF)
    for pt in range(8):
        for half in range(2):
            hrow = PHID[2 * pt + half]
            I_all[64 * half:64 * half + 64, 16 * pt + hrow] = 1
            IT_all[hrow, 128 * pt + 64 * half:128 * pt + 64 * half + 64] = 1
    selpair = np.zeros((16, 1024), dtype=np.float32)
    for o in range(8):
        selpair[PHID[2 * o], 128 * o:128 * o + 64] = 1
        selpair[PHID[2 * o + 1], 128 * o + 64:128 * o + 128] = 1
    return dict(
        ident16=np.eye(128, dtype=BF),
        identf=np.eye(128, dtype=np.float32),
        bigmask=(pp <= cc - 384).astype(BF),
        iall=I_all, itall=IT_all, selpair=selpair,
        onesf=np.ones((1, 128), dtype=np.float32),
        fconst=np.tile(np.array([[1e-5, 1e-10, -0.5, 0.125, 1.0 + 1e-5, 0.0, 1.0, 2.0]],
                                dtype=np.float32), (128, 1)),
        onescol_b=np.ones((128, 1), dtype=BF),
        ones128f=np.ones((128, 1), dtype=np.float32),
    )


def build_nc():
    _SIG.clear()
    nc = bass.Bass("TRN2", target_bir_lowering=False)

    def din(name, shape):
        return nc.dram_tensor(name, shape, F32, kind="ExternalInput").ap()

    x_in = din("x", [BL, T, D])
    wqkvf = din("wqkvf", [4 * D, D])
    wconv = din("wconv", [HS, HS, NF])
    wproj = din("wproj", [D, D])
    wff1 = din("wff1", [D, DH])
    wff2 = din("wff2", [DH, D])
    ln1g = din("ln1g", [D]); ln1b = din("ln1b", [D])
    ln2g = din("ln2g", [D]); ln2b = din("ln2b", [D])
    w1 = din("w1", [DH, D])
    convw = din("convw", [DH, 1, 3])
    convb = din("convb", [DH])
    w2 = din("w2", [D, DH])
    out_d = nc.dram_tensor("out", [BL, T, D], F32, kind="ExternalOutput").ap()

    def scr(name, shape, dt):
        return nc.dram_tensor(name, shape, dt, kind="Internal").ap()

    wqkvfT_s = scr("wqkvfT_s", [32, 8, 128, 128], BF16)
    wprojT_s = scr("wprojT_s", [8, 128, 1024], BF16)
    m1T_s = scr("m1T_s", [32, 8, 128, 128], BF16)
    m2T_s = scr("m2T_s", [32, 128, 1024], BF16)
    qT_s = scr("qT_s", [8, 128, T], BF16)
    fT_s = scr("fT_s", [8, 128, T], BF16)
    u_s = scr("u_s", [NT, 128, D], F32)
    u1_s = scr("u1_s", [NT, 128, D], F32)
    xa_s = scr("xa_s", [NT, 128, D], F32)
    h2w_s = scr("h2w_s", [32, 128, T], F32)
    n2_s = scr("n2_s", [32, 128, T], BF16)
    dct_s = scr("dct_s", [2, 1, 4096], F32)

    cn = _consts()
    cd = {k: nc.inline_tensor(v, k + "_c") for k, v in cn.items()}

    es = contextlib.ExitStack()

    def sb(name, shape, dt=F32):
        return es.enter_context(nc.sbuf_tensor(name, shape, dt))

    ident16 = sb("ident16", [128, 128], BF16)
    identf = sb("identf", [128, 128], F32)
    bigmask = sb("bigmask", [128, 896], BF16)
    iall = sb("iall", [128, 128], BF16)
    itall = sb("itall", [16, 1024], BF16)
    selpair = sb("selpair", [16, 1024], F32)
    onesf = sb("onesf", [1, 128], F32)
    onescol_b = sb("onescol_b", [128, 1], BF16)
    ones128f = sb("ones128f", [128, 1], F32)
    convb_pp = sb("convb_pp", [128, 32], F32)
    convw_pp = sb("convw_pp", [128, 96], F32)
    bdw = sb("bdw", [128, 512], BF16)
    wconv_sb = sb("wconv_sb", [128, 256], F32)
    cvstage = sb("cvstage", [128, 512], F32)
    lnrow = sb("lnrow", [1, D], F32)
    gb_bc = sb("gb_bc", [128, 2 * D], F32)
    meanw = sb("meanw", [128, 2], F32)
    kmax2 = sb("kmax2", [16, 1], F32)
    stats = sb("stats", [128, 8], F32)
    smallA = sb("smallA", [128, 64], F32)
    smallB = sb("smallB", [128, 64], F32)
    tbufA = sb("tbufA", [128, 128], F32)
    fconst = sb("fconst", [128, 8], F32)
    for _i, _v in enumerate([float(EPS), float(EPS * EPS), -0.5, 0.125, float(1.0 + EPS)]):
        nc.const_aps.register(F32, float(_v), fconst[:, _i:_i + 1])
    gam1 = sb("gam1", [128, 16], F32)
    gam2 = sb("gam2", [128, 16], F32)

    dS = Ctr(es, nc, "dS"); dG = Ctr(es, nc, "dG"); dC = Ctr(es, nc, "dC")
    dT = Ctr(es, nc, "dT")
    cT = Ctr(es, nc, "cT"); cV = Ctr(es, nc, "cV"); cS = Ctr(es, nc, "cS")
    cG = Ctr(es, nc, "cG")

    _uid = [0]

    def _un(name):
        _uid[0] += 1
        return f"{name}_{_uid[0]}"

    blk = es.enter_context(nc.Block())

    _eng_ctr = {"tensor": (cT, ()), "vector": (cV, ()),
                "scalar": (cS, (dC,)), "gpsimd": (cG, (dG,)),
                "sync": (None, (dS, dT))}

    def on(eng, fn):
        c, dctrs = _eng_ctr[eng]
        getattr(blk, eng)(lambda e: fn(SerEng(e, c, dctrs, coarse=(eng == "tensor"))))

    def VE(v):
        pass

    def barrier():
        for c in (cT, cV, cS, cG):
            if c.last is not None and c.last not in _SIG:
                c.inc(c.last)
        vals = [(dS, dS.n), (dG, dG.n), (dC, dC.n), (dT, dT.n), (cT, cT.n),
                (cV, cV.n), (cS, cS.n), (cG, cG.n)]

        def fb(e, vals=vals):
            for ctr, val in vals:
                e.wait_ge(ctr.h, val)
        for eng in ["sync", "gpsimd", "tensor", "vector", "scalar"]:
            on(eng, fb)
            c, dctrs = _eng_ctr[eng]
            wm = c.wmap if c is not None else dctrs[0].wmap
            for ctr, val in vals:
                wm[ctr] = max(wm.get(ctr, 0), val)
            for d in dctrs:
                dwm = c.wmap if c is not None else d.wmap
                dwm[("dma", d)] = max(dwm.get(("dma", d), 0), d.n)

    def s0_consts():
        def f(sync):
            for dst, key in [(ident16, "ident16"), (identf, "identf"),
                             (bigmask, "bigmask"), (iall, "iall"), (itall, "itall"),
                             (selpair, "selpair"), (onesf, "onesf"),
                             (fconst, "fconst"),
                             (onescol_b, "onescol_b"), (ones128f, "ones128f")]:
                dS.dma(sync.dma_start(out=dst[:], in_=cd[key].ap()))
            dS.dma(sync.dma_start(out=wconv_sb[0:64, :],
                                  in_=wconv.rearrange("c d f -> c (d f)")))
            dS.dma(sync.dma_start(out=wconv_sb[64:128, :],
                                  in_=wconv.rearrange("c d f -> c (d f)")))
            dS.dma(sync.dma_start(out=cvstage[0:32, 0:128],
                                  in_=convb.rearrange("(j p) -> j p", p=128)))
            dS.dma(sync.dma_start(
                out=cvstage[0:32, 128:512],
                in_=convw.rearrange("(j p) o k -> j (p o k)", p=128)))

        on("sync", f)
        ld0 = dS.n
        ps0es = contextlib.ExitStack()
        ps0 = ps0es.enter_context(nc.psum_tensor("ps0c", [128, 128], F32))

        def ftc0(t):
            t.wait_ge(dS.h, ld0)
            cT.inc(t.transpose(ps0[:, 0:32], cvstage[0:32, 0:128], identf[0:32, 0:32]))
        on("tensor", ftc0)
        tcn = cT.n

        def fvc0(v, tcn=tcn):
            v.wait_ge(cT.h, tcn)
            v.tensor_copy(convb_pp[:], ps0[:, 0:32])
            cV.inc(v.engine_nop())
        on("vector", fvc0)
        vcn = cV.n
        for k3 in range(3):
            def ftc1(t, k3=k3, vcn=vcn):
                t.wait_ge(cV.h, vcn)
                cT.inc(t.transpose(
                    ps0[:, 0:32],
                    cvstage[0:32, 128 + k3:512:3], identf[0:32, 0:32]))
            on("tensor", ftc1)
            tcn = cT.n

            def fvc1(v, k3=k3, tcn=tcn):
                v.wait_ge(cT.h, tcn)
                v.tensor_copy(convw_pp[:].rearrange("p (j k) -> p k j", k=3)[:, k3],
                              ps0[:, 0:32])
                cV.inc(v.engine_nop())
            on("vector", fvc1)
            vcn = cV.n

        def fv(v):
            v.wait_ge(dS.h, ld0)
            v.memset(bdw[:], 0.0)
            for fi in range(NF):
                v.tensor_copy(bdw[0:64, 128 * fi:128 * fi + 64],
                              wconv_sb[0:64, fi:256:4])
                v.tensor_copy(bdw[64:128, 128 * fi + 64:128 * fi + 128],
                              wconv_sb[64:128, fi:256:4])
            cV.inc(v.engine_nop())
        on("vector", fv)
        ps0es.close()

    def s0_wprep():
        with contextlib.ExitStack() as ps:
            def psb(name, shape, dt=F32):
                return ps.enter_context(nc.sbuf_tensor(_un(name), shape, dt))
            wrow = psb("p0_wrow", [128, 4096], F32)
            wtmp = psb("p0_wtmp", [128, 4096], F32)
            wi32 = psb("p0_wi32", [128, 4096], I32)
            wbf = psb("p0_wbf", [128, 4096], BF16)
            stage = psb("p0_stage", [128, 4096], BF16)
            acc = psb("p0_acc", [128, 2], F32)
            psT = ps.enter_context(nc.psum_tensor(_un("ps0T"), [128, 1024], BF16))
            psF = ps.enter_context(nc.psum_tensor(_un("ps0F"), [128, 512], F32))

            for col, (w_hbm, ntile, width) in enumerate([(w1, 32, D), (w2, 8, DH)]):
                for rt in range(ntile):
                    vn = cV.n

                    def fs(sync, rt=rt, w_hbm=w_hbm, width=width, vn=vn):
                        sync.wait_ge(cV.h, vn)
                        dS.dma(sync.dma_start(out=wrow[:, 0:width],
                                              in_=w_hbm[128 * rt:128 * rt + 128, :]))
                    on("sync", fs)
                    ldn = dS.n

                    def fv(v, rt=rt, width=width, col=col, ldn=ldn):
                        v.wait_ge(dS.h, ldn)
                        v.reduce_sum(smallA[:, 0:1], wrow[:, 0:width], AX,
                                     apply_absolute_value=True)
                        VE(v)
                        if rt == 0:
                            v.tensor_copy(acc[:, col:col + 1], smallA[:, 0:1])
                        else:
                            v.tensor_add(acc[:, col:col + 1], acc[:, col:col + 1],
                                         smallA[:, 0:1])
                        cV.inc(v.engine_nop())
                    on("vector", fv)
            vn = cV.n

            def ft(t, vn=vn):
                t.wait_ge(cV.h, vn)
                cT.inc(t.matmul(psF[0:2, 0:1], acc[:], ones128f[:],
                                start=True, stop=True))
            on("tensor", ft)
            tn = cT.n

            def fv2(v, tn=tn):
                v.wait_ge(cT.h, tn)
                v.tensor_scalar(smallA[0:2, 0:1], psF[0:2, 0:1],
                                float(1.0 / (DH * D)), 1e-5, ALU.mult, ALU.max)
                v.memset(tbufA[:], 0.0)
                VE(v)
                v.tensor_copy(tbufA[0:2, 0:1], smallA[0:2, 0:1])
                cV.inc(v.engine_nop())
            on("vector", fv2)
            vn = cV.n

            def ft2(t, vn=vn):
                t.wait_ge(cV.h, vn)
                cT.inc(t.transpose(psF[:, 0:128], tbufA[:], identf[:]))
            on("tensor", ft2)
            tn = cT.n

            def fv3(v, tn=tn):
                v.wait_ge(cT.h, tn)
                v.tensor_copy(smallB[0:1, 0:2], psF[0:1, 0:2])
                cV.inc(v.engine_nop())
            on("vector", fv3)
            vn = cV.n

            def ft3(t, vn=vn):
                t.wait_ge(cV.h, vn)
                cT.inc(t.matmul(psF[:, 0:2], onesf[:], smallB[0:1, 0:2],
                                start=True, stop=True))
            on("tensor", ft3)
            tn = cT.n

            def fv4(v, tn=tn):
                v.wait_ge(cT.h, tn)
                v.tensor_copy(meanw[:], psF[:, 0:2])
                VE(v)
                v.reciprocal(smallB[:, 2:4], meanw[:])
                cV.inc(v.engine_nop())
            on("vector", fv4)
            recip_ready = cV.n

            jobs = [("m1", w1, 32, D, None), ("m2", w2, 8, DH, None),
                    ("qkvf", wqkvf, 32, D, None), ("proj", wproj, 8, D, None)]
            stage_rd = [0]
            for which, w_hbm, ntile, width, _ in jobs:
                tern = which in ("m1", "m2")
                for rt in range(ntile):
                    vn = cV.n
                    if tern:
                        def fs(sync, rt=rt, w_hbm=w_hbm, width=width, vn=vn):
                            sync.wait_ge(cV.h, vn)
                            dS.dma(sync.dma_start(
                                out=wrow[:, 0:width],
                                in_=w_hbm[128 * rt:128 * rt + 128, :]))
                        on("sync", fs)
                        ldn = dS.n
                        col = 0 if which == "m1" else 1

                        def fv(v, width=width, col=col, ldn=ldn):
                            v.wait_ge(dS.h, ldn)
                            v.wait_ge(cV.h, recip_ready)
                            v.tensor_scalar_mul(wtmp[:, 0:width], wrow[:, 0:width],
                                                smallB[:, 2 + col:3 + col])
                            v.tensor_copy(wi32[:, 0:width], wtmp[:, 0:width])
                            v.tensor_scalar(wbf[:, 0:width], wi32[:, 0:width],
                                            -1.0, 1.0, ALU.max, ALU.min)
                            cV.inc(v.engine_nop())
                        on("vector", fv)
                    else:
                        def fg(g, rt=rt, w_hbm=w_hbm, width=width, vn=vn):
                            g.wait_ge(cV.h, vn)
                            dG.dma(g.dma_start(out=wbf[:, 0:width],
                                               in_=w_hbm[128 * rt:128 * rt + 128, :]))
                        on("gpsimd", fg)
                        ldn = dG.n

                        def fv(v, ldn=ldn):
                            v.wait_ge(dG.h, ldn)
                            cV.inc(v.engine_nop())
                        on("vector", fv)
                    ngrp = width // 1024
                    for g in range(ngrp):
                        vn = cV.n

                        def ftr(t, g=g, vn=vn):
                            t.wait_ge(cV.h, vn)
                            for j in range(8):
                                _li = t.transpose(
                                    psT[:, 128 * j:128 * j + 128],
                                    wbf[:, 1024 * g + 128 * j:1024 * g + 128 * j + 128],
                                    ident16[:])
                            cT.inc(_li)
                        on("tensor", ftr)
                        tn = cT.n

                        def fev(v, g=g, tn=tn, srd=stage_rd[0]):
                            v.wait_ge(cT.h, tn)
                            v.wait_ge(dT.h, srd)
                            v.tensor_copy(stage[:, 1024 * g:1024 * g + 1024], psT[:])
                            cV.inc(v.engine_nop())
                        on("vector", fev)
                    sn = cV.n

                    def fst(sync, rt=rt, which=which, sn=sn):
                        sync.wait_ge(cV.h, sn)
                        if which == "m1":
                            dT.dma(sync.dma_start(
                                out=m1T_s[rt].rearrange("j p c -> p j c"),
                                in_=stage[:, 0:1024].rearrange("p (j c) -> p j c", j=8)))
                        elif which == "m2":
                            dT.dma(sync.dma_start(
                                out=m2T_s[:, :, 128 * rt:128 * rt + 128]
                                .rearrange("j p o -> p j o"),
                                in_=stage[:].rearrange("p (j o) -> p j o", j=32)))
                        elif which == "qkvf":
                            dT.dma(sync.dma_start(
                                out=wqkvfT_s[rt].rearrange("j p c -> p j c"),
                                in_=stage[:, 0:1024].rearrange("p (j c) -> p j c", j=8)))
                        else:
                            dT.dma(sync.dma_start(
                                out=wprojT_s[:, :, 128 * rt:128 * rt + 128]
                                .rearrange("j p o -> p j o"),
                                in_=stage[:, 0:1024].rearrange("p (j o) -> p j o", j=8)))
                    on("sync", fst)
                    stage_rd[0] = dT.n

    def s1(b, ktT, vbuf):
        with contextlib.ExitStack() as ps:
            def psb(name, shape, dt=F32):
                return ps.enter_context(nc.sbuf_tensor(_un(name), shape, dt))
            h1T = psb("s1_h1T", [128, 8 * T], BF16)
            xt = psb("s1_xt", [128, 2 * D], F32)
            sq = psb("s1_sq", [128, D], F32)
            ut = psb("s1_ut", [128, D], F32)
            ta = psb("s1_ta", [128, D], F32)
            h1f = psb("s1_h1f", [128, D], F32)
            wring = psb("s1_wring", [128, 2 * 1024], BF16)
            wvfull = psb("s1_wvfull", [128, 8 * 1024], BF16)
            qfst = psb("s1_qfst", [128, 2 * 2048], BF16)
            sqk = psb("s1_sqk", [128, 2048], BF16)
            psW0 = ps.enter_context(nc.psum_tensor(_un("s1_psW0"), [128, 1024], F32))
            psW1 = ps.enter_context(nc.psum_tensor(_un("s1_psW1"), [128, 1024], F32))
            psTP = ps.enter_context(nc.psum_tensor(_un("s1_psTP"), [128, 1024], F32))
            psKN = ps.enter_context(nc.psum_tensor(_un("s1_psKN"), [16, 512], F32))

            for half, row_hbm in [(0, ln1g), (1, ln1b)]:
                def flr(sync, row_hbm=row_hbm, _tp=cT.n):
                    sync.wait_ge(cT.h, _tp)
                    dS.dma(sync.dma_start(out=lnrow[:], in_=row_hbm[None, :]))
                on("sync", flr)
                ldr = dS.n

                def ftg(t, half=half, ldr=ldr):
                    t.wait_ge(dS.h, ldr)
                    pdst = psW0 if half == 0 else psW1
                    for n in range(2):
                        cT.inc(t.matmul(
                            pdst[:, 512 * n:512 * n + 512], onesf[:],
                            lnrow[0:1, 512 * n:512 * n + 512],
                            start=True, stop=True))
                on("tensor", ftg)
            tn = cT.n

            def fvg(v, tn=tn):
                v.wait_ge(cT.h, tn)
                v.tensor_copy(gb_bc[:, 0:D], psW0[:])
                v.tensor_copy(gb_bc[:, D:2 * D], psW1[:])
                cV.inc(v.engine_nop())
            on("vector", fvg)
            gbn = cV.n

            ev_done = {}
            for i in range(NT):
                prevev = ev_done.get(i - 2, 0)

                def fs(sync, i=i, b=b, prevev=prevev):
                    sync.wait_ge(cV.h, prevev)
                    dS.dma(sync.dma_start(out=xt[:, D * (i % 2):D * (i % 2) + D],
                                          in_=x_in[b, 128 * i:128 * i + 128, :]))
                on("sync", fs)
                ldn = dS.n
                xs = xt[:, D * (i % 2):D * (i % 2) + D]

                def fv1(v, xs=xs, ldn=ldn, _sn=cS.n):
                    v.wait_ge(dS.h, ldn)
                    v.wait_ge(cS.h, _sn)
                    v.reduce_sum(stats[:, 0:1], xs, AX)
                    v.tensor_mul(sq[:], xs, xs)
                    v.reduce_sum(stats[:, 1:2], sq[:], AX)
                    v.tensor_scalar_mul(stats[:, 2:3], stats[:, 0:1], float(1.0 / D))
                    v.tensor_scalar_mul(stats[:, 3:4], stats[:, 1:2], float(1.0 / D))
                    VE(v)
                    v.tensor_mul(stats[:, 4:5], stats[:, 2:3], stats[:, 2:3])
                    VE(v)
                    v.tensor_sub(stats[:, 3:4], stats[:, 3:4], stats[:, 4:5])
                    cV.inc(v.engine_nop())
                on("vector", fv1)
                vn = cV.n

                def fs1(s, i=i, vn=vn):
                    s.wait_ge(cV.h, vn)
                    _li = s.activation(stats[:, 4:5], stats[:, 3:4], ACT.Ln, bias=float(EPS))
                    _li = s.activation(smallA[:, i:i + 1], stats[:, 4:5], ACT.Exp, scale=-0.5)
                    _li = s.activation(stats[:, 5:6], stats[:, 3:4], ACT.Ln,
                                 bias=float(EPS * EPS), scale=float(1.0 + EPS))
                    _li = s.activation(smallB[:, i:i + 1], stats[:, 5:6], ACT.Exp, scale=-0.5)
                    cS.inc(_li)
                on("scalar", fs1)
                sn = cS.n

                def fv2(v, i=i, sn=sn):
                    v.wait_ge(cS.h, sn)
                    v.scalar_tensor_tensor(smallA[:, 16 + i:17 + i], stats[:, 2:3],
                                           -1.0, smallA[:, i:i + 1], ALU.mult, ALU.mult)
                    v.scalar_tensor_tensor(smallB[:, 16 + i:17 + i], stats[:, 2:3],
                                           -1.0, smallB[:, i:i + 1], ALU.mult, ALU.mult)
                    cV.inc(v.engine_nop())
                on("vector", fv2)
                vn = cV.n

                def fv3(v, i=i, xs=xs, vn=vn, gbn=gbn, _tp=cT.n, _dp=dS.n):
                    v.wait_ge(cV.h, vn)
                    v.wait_ge(cV.h, gbn)
                    v.wait_ge(cT.h, _tp)
                    v.wait_ge(dS.h, _dp)
                    v.tensor_scalar(ut[:], xs, smallA[:, i:i + 1],
                                    smallA[:, 16 + i:17 + i], ALU.mult, ALU.add)
                    v.tensor_scalar(ta[:], xs, smallB[:, i:i + 1],
                                    smallB[:, 16 + i:17 + i], ALU.mult, ALU.add)
                    v.tensor_mul(ta[:], ta[:], gb_bc[:, 0:D])
                    v.tensor_add(h1f[:], ta[:], gb_bc[:, D:2 * D])
                    cV.inc(v.engine_nop())
                on("vector", fv3)
                vn = cV.n

                def fsu(sync, i=i, vn=vn):
                    sync.wait_ge(cV.h, vn)
                    dS.dma(sync.dma_start(out=u_s[i], in_=ut[:]))
                on("sync", fsu)
                prev_ev = ev_done.get(i - 1, 0)

                def ftr(t, vn=vn, prev_ev=prev_ev):
                    t.wait_ge(cV.h, vn)
                    t.wait_ge(cV.h, prev_ev)
                    for j in range(8):
                        _li = t.transpose(psTP[:, 128 * j:128 * j + 128],
                                    h1f[:, 128 * j:128 * j + 128], identf[:])
                    cT.inc(_li)
                on("tensor", ftr)
                tn = cT.n

                def fev(v, i=i, tn=tn):
                    v.wait_ge(cT.h, tn)
                    v.tensor_copy(
                        h1T[:].rearrange("p (j t) -> p j t", j=8)[:, :, 128 * i:128 * i + 128],
                        psTP[:].rearrange("p (j t) -> p j t", j=8))
                    cV.inc(v.engine_nop())
                    ev_done[i] = cV.n
                on("vector", fev)
            h1T_done = cV.n
            u_spill_done = dS.n

            ldprev = {}
            for grp, dst in [("q", None), ("k", None), ("f", None)]:
                base_o = {"q": 0, "k": 8, "f": 24}[grp]
                for o in range(8):
                    scn = cS.n

                    def fl(sync, o=o, base_o=base_o, scn=scn):
                        sync.wait_ge(cS.h, scn)
                        dS.dma(sync.dma_start(
                            out=wring[:, 1024 * (o % 2):1024 * (o % 2) + 1024]
                            .rearrange("p (k c) -> p k c", k=8),
                            in_=wqkvfT_s[base_o + o].rearrange("k p c -> p k c")))
                    on("sync", fl)
                    ldn = dS.n

                    def ft(t, o=o, ldn=ldn):
                        t.wait_ge(dS.h, ldn)
                        t.wait_ge(cV.h, h1T_done)
                        t.wait_ge(cS.h, cS.n)
                        for k in range(8):
                            lhsT = wring[:, 1024 * (o % 2) + 128 * k:
                                         1024 * (o % 2) + 128 * k + 128]
                            for n in range(4):
                                pdst = (psW0 if n < 2 else psW1)
                                _li = t.matmul(pdst[:, 512 * (n % 2):512 * (n % 2) + 512],
                                         lhsT,
                                         h1T[:, T * k + 512 * n:T * k + 512 * n + 512],
                                         start=(k == 0), stop=(k == 7))
                        cT.inc(_li)
                    on("tensor", ft)
                    tn = cT.n

                    def fe(s, o=o, grp=grp, tn=tn):
                        s.wait_ge(cT.h, tn)
                        if grp == "k":
                            _li = s.copy(ktT[:, T * o:T * o + 1024], psW0[:])
                            _li = s.copy(ktT[:, T * o + 1024:T * o + 2048], psW1[:])
                        else:
                            st = qfst[:, 2048 * (o % 2):2048 * (o % 2) + 2048]
                            _li = s.copy(st[:, 0:1024], psW0[:])
                            _li = s.copy(st[:, 1024:2048], psW1[:])
                        cS.inc(_li)
                    on("scalar", fe)
                    sn = cS.n
                    if grp != "k":
                        def fst(sync, o=o, grp=grp, sn=sn):
                            sync.wait_ge(cS.h, sn)
                            dst_s = qT_s if grp == "q" else fT_s
                            dS.dma(sync.dma_start(
                                out=dst_s[o],
                                in_=qfst[:, 2048 * (o % 2):2048 * (o % 2) + 2048]))
                        on("sync", fst)

            def flv(sync):
                for k in range(8):
                    dS.dma(sync.dma_start(
                        out=wvfull[:, 1024 * k:1024 * k + 1024]
                        .rearrange("p (j c) -> p j c", j=8),
                        in_=wqkvfT_s[16:24, k].rearrange("j p c -> p j c")))
            on("sync", flv)
            ldv = dS.n
            vones = cV.n

            def fvo(v):
                v.memset(vbuf[:].rearrange("p (ti h s) -> p ti h s", ti=NT, h=H)[:, :, :, 64:65], 1.0)
                cV.inc(v.engine_nop())
            on("vector", fvo)

            for ti in range(NT):
                scn = cS.n

                def ftv(t, ti=ti, scn=scn):
                    t.wait_ge(dS.h, ldv)
                    t.wait_ge(cV.h, h1T_done)
                    t.wait_ge(cS.h, scn)
                    for k in range(8):
                        lhsT = h1T[:, T * k + 128 * ti:T * k + 128 * ti + 128]
                        _li = t.matmul(psW0[:, 0:512], lhsT,
                                 wvfull[:, 1024 * k:1024 * k + 512],
                                 start=(k == 0), stop=(k == 7))
                        _li = t.matmul(psW0[:, 512:1024], lhsT,
                                 wvfull[:, 1024 * k + 512:1024 * k + 1024],
                                 start=(k == 0), stop=(k == 7))
                    cT.inc(_li)
                on("tensor", ftv)
                tn = cT.n

                def fve(s, ti=ti, tn=tn):
                    s.wait_ge(cT.h, tn)
                    _li = s.copy(vbuf[:].rearrange("p (ti h s) -> p ti h s", ti=NT, h=H)
                           [:, ti, :, 0:64],
                           psW0[:].rearrange("p (h s) -> p h s", h=H))
                    cS.inc(_li)
                on("scalar", fve)

            for pt in range(8):
                def fvk(v, pt=pt, _sn=cS.n, _tp=cT.n):
                    v.wait_ge(cS.h, _sn)
                    v.wait_ge(cT.h, _tp)
                    v.tensor_mul(sqk[:], ktT[:, T * pt:T * pt + T],
                                 ktT[:, T * pt:T * pt + T])
                    cV.inc(v.engine_nop())
                on("vector", fvk)
                for n in range(4):
                    vn = cV.n

                    def ftk(t, pt=pt, n=n, vn=vn, _vp=cV.n):
                        t.wait_ge(cV.h, _vp)
                        cT.inc(t.matmul(psKN[:], iall[:, 16 * pt:16 * pt + 16],
                                        sqk[:, 512 * n:512 * n + 512],
                                        start=True, stop=True))
                    on("tensor", ftk)
                    tn = cT.n

                    def fvkm(v, pt=pt, n=n, tn=tn):
                        v.wait_ge(cT.h, tn)
                        v.reduce_max(smallA[0:16, 8:9], psKN[:], AX)
                        VE(v)
                        if pt == 0 and n == 0:
                            v.tensor_copy(kmax2[:], smallA[0:16, 8:9])
                        else:
                            v.tensor_max(kmax2[:], kmax2[:], smallA[0:16, 8:9])
                        cV.inc(v.engine_nop())
                    on("vector", fvkm)

    def s2(b, ktT, vbuf):
        with contextlib.ExitStack() as ps:
            def psb(name, shape, dt=F32):
                return ps.enter_context(nc.sbuf_tensor(_un(name), shape, dt))
            qTc = psb("s2_qTc", [128, 8 * 512], BF16)
            fTc = psb("s2_fTc", [128, 8 * 512], BF16)
            upb = psb("s2_upb", [128, 8 * 512], BF16)
            kvro = psb("s2_kvro", [128, 512], BF16)
            Eb = psb("s2_Eb", [128, 512], BF16)
            expFA = psb("s2_expFA", [16, 4 * 512], BF16)
            sumFA = psb("s2_sumFA", [16, 512], F32)
            expS = psb("s2_expS", [128, 3 * 2048], BF16)
            attnC = psb("s2_attnC", [128, 8 * 512], F32)
            attnpre = psb("s2_attnpre", [128, 8 * 512], BF16)
            dccol = psb("s2_dccol", [128, 8 * 512], F32)
            dct = psb("s2_dct", [16, 512], F32)
            den16 = psb("s2_den16", [16, 512], F32)
            recip16 = psb("s2_recip16", [16, 512], F32)
            negc = psb("s2_negc", [16, 1], F32)
            negcb = psb("s2_negcb", [128, 16], F32)
            wring = psb("s2_wring", [128, 2 * 1024], BF16)
            u1t = psb("s2_u1t", [128, 1024], F32)
            fxt = psb("s2_fxt", [128, 512], F32)
            psW0 = ps.enter_context(nc.psum_tensor(_un("s2_psW0"), [128, 1024], F32))
            psW1 = ps.enter_context(nc.psum_tensor(_un("s2_psW1"), [128, 1024], F32))
            psE = ps.enter_context(nc.psum_tensor(_un("s2_psE"), [128, 512], F32))
            psO = ps.enter_context(nc.psum_tensor(_un("s2_psO"), [128, 512], F32))
            psD = ps.enter_context(nc.psum_tensor(_un("s2_psD"), [128, 512], F32))
            psX = ps.enter_context(nc.psum_tensor(_un("s2_psX"), [128, 512], F32))

            for c in range(NCH):
                nkt = 4 * c + 4
                vfree = cV.n
                tfree = cT.n

                def flq(sync, c=c, vfree=vfree, tfree=tfree):
                    sync.wait_ge(cV.h, vfree)
                    sync.wait_ge(cT.h, tfree)
                    dS.dma(sync.dma_start(
                        out=qTc[:].rearrange("p (o t) -> p o t", o=8),
                        in_=qT_s[:, :, 512 * c:512 * c + 512]
                        .rearrange("o p t -> p o t")))
                    dS.dma(sync.dma_start(
                        out=fTc[:].rearrange("p (o t) -> p o t", o=8),
                        in_=fT_s[:, :, 512 * c:512 * c + 512]
                        .rearrange("o p t -> p o t")))
                on("sync", flq)
                ldqf = dS.n

                for fi in range(NF):
                    pFA = psW0 if fi < 2 else psW1
                    sl = slice(512 * (fi % 2), 512 * (fi % 2) + 512)
                    for pt in range(8):
                        vn = cV.n
                        scn = cS.n

                        def ftu(t, fi=fi, pt=pt, vn=vn, scn=scn):
                            t.wait_ge(dS.h, ldqf)
                            t.wait_ge(cS.h, scn)
                            t.wait_ge(cV.h, vn)
                            _li = t.matmul(psX[:], bdw[:, 128 * fi:128 * fi + 128],
                                     fTc[:, 512 * pt:512 * pt + 512],
                                     start=True, stop=True)
                            cT.inc(_li)
                        on("tensor", ftu)
                        tn = cT.n

                        def fue(s, pt=pt, tn=tn):
                            s.wait_ge(cT.h, tn)
                            _li = s.copy(upb[:, 512 * pt:512 * pt + 512], psX[:])
                            cS.inc(_li)
                        on("scalar", fue)
                    for o in range(8):
                        vn = cV.n

                        def flw(sync, o=o, vn=vn):
                            sync.wait_ge(cV.h, vn)
                            dS.dma(sync.dma_start(
                                out=wring[:, 1024 * (o % 2):1024 * (o % 2) + 1024]
                                .rearrange("p (k c) -> p k c", k=8),
                                in_=wqkvfT_s[8 + o].rearrange("k p c -> p k c")))
                        on("sync", flw)
                        ldw = dS.n
                        vn = cV.n
                        scn = cS.n

                        def ftk(t, o=o, fi=fi, ldw=ldw, vn=vn, scn=scn, pFA=pFA, sl=sl):
                            t.wait_ge(dS.h, ldw)
                            t.wait_ge(cV.h, vn)
                            t.wait_ge(cS.h, scn)
                            for k in range(8):
                                _li = t.matmul(psX[:],
                                         wring[:, 1024 * (o % 2) + 128 * k:
                                               1024 * (o % 2) + 128 * k + 128],
                                         upb[:, 512 * k:512 * k + 512],
                                         start=(k == 0), stop=(k == 7))
                            cT.inc(_li)
                        on("tensor", ftk)
                        tn = cT.n

                        def fke(v, o=o, tn=tn):
                            v.wait_ge(cT.h, tn)
                            v.tensor_copy(kvro[:], psX[:])
                            VE(v)
                            v.tensor_mul(Eb[:], qTc[:, 512 * o:512 * o + 512], kvro[:])
                            cV.inc(v.engine_nop())
                        on("vector", fke)
                        vn = cV.n

                        def ftf(t, o=o, vn=vn, pFA=pFA, sl=sl):
                            t.wait_ge(cV.h, vn)
                            _li = t.matmul(pFA[0:16, sl], iall[:, 16 * o:16 * o + 16],
                                     Eb[:], start=(o == 0), stop=(o == 7))
                            cT.inc(_li)
                        on("tensor", ftf)

                prev_tq = cT.n
                for o in range(8):
                    vn = cV.n

                    def fvq(v, o=o, prev_tq=prev_tq):
                        v.wait_ge(dS.h, ldqf)
                        v.wait_ge(cT.h, prev_tq)
                        v.tensor_mul(Eb[:], qTc[:, 512 * o:512 * o + 512],
                                     qTc[:, 512 * o:512 * o + 512])
                        cV.inc(v.engine_nop())
                    on("vector", fvq)
                    vn = cV.n

                    def ftq(t, o=o, vn=vn):
                        t.wait_ge(cV.h, vn)
                        _li = t.matmul(psE[0:16, :], iall[:, 16 * o:16 * o + 16], Eb[:],
                                 start=(o == 0), stop=(o == 7))
                        cT.inc(_li)
                    on("tensor", ftq)
                    prev_tq = cT.n
                tn = cT.n

                def fvc(v, tn=tn):
                    v.wait_ge(cT.h, tn)
                    v.reduce_max(smallA[0:16, 0:1], psE[0:16, :], AX)
                    v.reduce_max(smallA[0:16, 2:3], psW0[0:16, 0:512], AX)
                    v.reduce_max(smallA[0:16, 3:4], psW0[0:16, 512:1024], AX)
                    v.reduce_max(smallA[0:16, 4:5], psW1[0:16, 0:512], AX)
                    v.reduce_max(smallA[0:16, 5:6], psW1[0:16, 512:1024], AX)
                    v.memset(tbufA[:], 0.0)
                    VE(v)
                    v.tensor_add(smallA[0:16, 1:2], smallA[0:16, 0:1], kmax2[:])
                    v.tensor_max(smallA[0:16, 2:3], smallA[0:16, 2:3], smallA[0:16, 3:4])
                    v.tensor_max(smallA[0:16, 4:5], smallA[0:16, 4:5], smallA[0:16, 5:6])
                    VE(v)
                    v.tensor_scalar_mul(smallA[0:16, 1:2], smallA[0:16, 1:2], 0.0625)
                    v.tensor_max(smallA[0:16, 2:3], smallA[0:16, 2:3], smallA[0:16, 4:5])
                    VE(v)
                    v.tensor_max(smallA[0:16, 1:2], smallA[0:16, 1:2], smallA[0:16, 2:3])
                    VE(v)
                    v.tensor_scalar_mul(negc[:], smallA[0:16, 1:2], -1.0)
                    VE(v)
                    v.tensor_copy(tbufA[0:16, 0:1], negc[:])
                    cV.inc(v.engine_nop())
                on("vector", fvc)
                vn = cV.n

                def ftc(t, vn=vn):
                    t.wait_ge(cV.h, vn)
                    _li = t.transpose(psX[:, 0:128], tbufA[:], identf[:])
                    cT.inc(_li)
                on("tensor", ftc)
                tn = cT.n

                def fvc2(v, tn=tn):
                    v.wait_ge(cT.h, tn)
                    v.tensor_copy(smallB[0:1, 0:16], psX[0:1, 0:16])
                    cV.inc(v.engine_nop())
                on("vector", fvc2)
                vn = cV.n

                def ftc2(t, vn=vn):
                    t.wait_ge(cV.h, vn)
                    cT.inc(t.matmul(psX[:, 0:16], onesf[:], smallB[0:1, 0:16],
                                    start=True, stop=True))
                on("tensor", ftc2)
                tn = cT.n

                def fvc3(v, tn=tn):
                    v.wait_ge(cT.h, tn)
                    v.tensor_copy(negcb[:], psX[:, 0:16])
                    cV.inc(v.engine_nop())
                on("vector", fvc3)
                vncb = cV.n

                def fse(s, vncb=vncb):
                    s.wait_ge(cV.h, vncb)
                    for fi in range(NF):
                        pFA = psW0 if fi < 2 else psW1
                        sl = slice(512 * (fi % 2), 512 * (fi % 2) + 512)
                        _li = s.activation(expFA[:, 512 * fi:512 * fi + 512],
                                     pFA[0:16, sl], ACT.Exp, bias=negc[:])
                    cS.inc(_li)
                on("scalar", fse)
                sn = cS.n

                def fvsf(v, sn=sn):
                    v.wait_ge(cS.h, sn)
                    v.tensor_add(den16[:], expFA[:, 0:512], expFA[:, 512:1024])
                    v.tensor_add(sumFA[:], expFA[:, 1024:1536], expFA[:, 1536:2048])
                    VE(v)
                    v.tensor_add(sumFA[:], sumFA[:], den16[:])
                    cV.inc(v.engine_nop())
                on("vector", fvsf)
                sumfa_n = cV.n

                for o in range(8):
                    h1r, h2r = PHID[2 * o], PHID[2 * o + 1]
                    for ktp in range(0, nkt, 2):
                        ring = (ktp // 2) % 3
                        exn = cS.n
                        vmn = cV.n
                        pvn = cT.n

                        def fts(t, o=o, ktp=ktp, exn=exn):
                            t.wait_ge(cS.h, exn)
                            for kd in range(2):
                                kt = ktp + kd
                                _li = t.matmul(psW0[:, 512 * kd:512 * kd + 512],
                                         ktT[0:64, T * o + 128 * kt:T * o + 128 * kt + 128],
                                         qTc[0:64, 512 * o:512 * o + 512],
                                         start=True, stop=True)
                                _li = t.matmul(psW1[:, 512 * kd:512 * kd + 512],
                                         ktT[64:128, T * o + 128 * kt:T * o + 128 * kt + 128],
                                         qTc[64:128, 512 * o:512 * o + 512],
                                         start=True, stop=True)
                            cT.inc(_li)
                        on("tensor", fts)
                        tn = cT.n

                        def fex(s, o=o, ring=ring, h1r=h1r, h2r=h2r, tn=tn, vmn=vmn):
                            s.wait_ge(cT.h, tn)
                            s.wait_ge(cV.h, vmn)
                            _li = s.activation(expS[:, 2048 * ring:2048 * ring + 1024],
                                         psW0[:], ACT.Exp, scale=0.125,
                                         bias=negcb[:, h1r:h1r + 1])
                            _li = s.activation(expS[:, 2048 * ring + 1024:2048 * ring + 2048],
                                         psW1[:], ACT.Exp, scale=0.125,
                                         bias=negcb[:, h2r:h2r + 1])
                            cS.inc(_li)
                        on("scalar", fex)
                        sn = cS.n
                        mask_ops = []
                        for kd in range(2):
                            kt = ktp + kd
                            roff = 128 * kt - 512 * c
                            if roff >= 0:
                                mask_ops.append((kd, roff))
                        if mask_ops:
                            def fmk(v, ring=ring, mask_ops=mask_ops, sn=sn):
                                v.wait_ge(cS.h, sn)
                                for kd, roff in mask_ops:
                                    for hh in range(2):
                                        sl2 = slice(2048 * ring + 1024 * hh + 512 * kd,
                                                    2048 * ring + 1024 * hh + 512 * kd + 512)
                                        v.tensor_mul(expS[:, sl2], expS[:, sl2],
                                                     bigmask[:, 384 - roff:896 - roff])
                                cV.inc(v.engine_nop())
                            on("vector", fmk)
                        mkn = cV.n

                        def fpv(t, o=o, ktp=ktp, ring=ring, mask_ops=mask_ops,
                                sn=sn, mkn=mkn, nkt=nkt):
                            if mask_ops:
                                t.wait_ge(cV.h, mkn)
                            else:
                                t.wait_ge(cS.h, sn)
                            for kd in range(2):
                                kt = ktp + kd
                                e1 = expS[:, 2048 * ring + 512 * kd:
                                          2048 * ring + 512 * kd + 512]
                                e2 = expS[:, 2048 * ring + 1024 + 512 * kd:
                                          2048 * ring + 1024 + 512 * kd + 512]
                                vb = vbuf[:, 1040 * kt + 65 * (2 * o):
                                          1040 * kt + 65 * (2 * o) + 65]
                                vb2 = vbuf[:, 1040 * kt + 65 * (2 * o + 1):
                                           1040 * kt + 65 * (2 * o + 1) + 64]
                                _li = t.matmul(psE[0:65, :], vb, e1,
                                         start=(kt == 0), stop=(kt == nkt - 1))
                                _li = t.matmul(psO[64:128, :], vb2, e2,
                                         start=(kt == 0), stop=(kt == nkt - 1),
                                         tile_position=(0, 64))
                                _li = t.matmul(psD[0:1, :], onescol_b[:], e2,
                                         start=(kt == 0), stop=(kt == nkt - 1))
                            cT.inc(_li)
                        on("tensor", fpv)
                    tn = cT.n

                    def fpe(s, o=o, tn=tn):
                        s.wait_ge(cT.h, tn)
                        _li = s.copy(attnC[0:64, 512 * o:512 * o + 512], psE[0:64, :])
                        _li = s.copy(attnC[64:128, 512 * o:512 * o + 512], psO[64:128, :])
                        cS.inc(_li)
                    on("scalar", fpe)

                    def fpd(v, o=o, tn=tn, _dp=dS.n):
                        v.wait_ge(cT.h, tn)
                        v.wait_ge(dS.h, _dp)
                        v.tensor_copy(dccol[64:65, 512 * o:512 * o + 512], psE[64:65, :])
                        v.tensor_copy(dccol[0:1, 512 * o:512 * o + 512], psD[0:1, :])
                        cV.inc(v.engine_nop())
                    on("vector", fpd)
                vdn = cV.n
                pairs_scalar = cS.n

                def fdd(sync, vdn=vdn):
                    sync.wait_ge(cV.h, vdn)
                    dS.dma(sync.dma_start(out=dct_s[0], in_=dccol[64:65, :]))
                    dS.dma(sync.dma_start(out=dct_s[1], in_=dccol[0:1, :]))
                on("sync", fdd)
                dd0 = dS.n

                def fdd2(sync, dd0=dd0):
                    sync.wait_ge(dS.h, dd0)
                    dS.dma(sync.dma_start(
                        out=dct[0:8, :],
                        in_=dct_s[0].rearrange("q (o t) -> (q o) t", o=8)))
                    dS.dma(sync.dma_start(
                        out=dct[8:16, :],
                        in_=dct_s[1].rearrange("q (o t) -> (q o) t", o=8)))
                on("sync", fdd2)
                ddn = dS.n

                def fdr(v, ddn=ddn):
                    v.wait_ge(dS.h, ddn)
                    v.wait_ge(cV.h, sumfa_n)
                    v.tensor_add(den16[:], dct[:], sumFA[:])
                    VE(v)
                    v.reciprocal(recip16[:], den16[:])
                    cV.inc(v.engine_nop())
                on("vector", fdr)
                rcn = cV.n

                for fi in range(NF):
                    for pt in range(8):
                        scn = cS.n

                        def ftu2(t, fi=fi, pt=pt, scn=scn):
                            t.wait_ge(cS.h, scn)
                            _li = t.matmul(psX[:], bdw[:, 128 * fi:128 * fi + 128],
                                     fTc[:, 512 * pt:512 * pt + 512],
                                     start=True, stop=True)
                            cT.inc(_li)
                        on("tensor", ftu2)
                        tn = cT.n

                        def fue2(s, pt=pt, tn=tn):
                            s.wait_ge(cT.h, tn)
                            _li = s.copy(upb[:, 512 * pt:512 * pt + 512], psX[:])
                            cS.inc(_li)
                        on("scalar", fue2)
                    for o in range(8):
                        vn = cV.n

                        def flw2(sync, o=o, vn=vn):
                            sync.wait_ge(cV.h, vn)
                            dS.dma(sync.dma_start(
                                out=wring[:, 1024 * (o % 2):1024 * (o % 2) + 1024]
                                .rearrange("p (k c) -> p k c", k=8),
                                in_=wqkvfT_s[16 + o].rearrange("k p c -> p k c")))
                        on("sync", flw2)
                        ldw = dS.n
                        vn = cV.n
                        scn = cS.n

                        def ftv2(t, o=o, ldw=ldw, vn=vn, scn=scn):
                            t.wait_ge(dS.h, ldw)
                            t.wait_ge(cV.h, vn)
                            t.wait_ge(cS.h, scn)
                            for k in range(8):
                                _li = t.matmul(psX[:],
                                         wring[:, 1024 * (o % 2) + 128 * k:
                                               1024 * (o % 2) + 128 * k + 128],
                                         upb[:, 512 * k:512 * k + 512],
                                         start=(k == 0), stop=(k == 7))
                            cT.inc(_li)
                        on("tensor", ftv2)
                        tn = cT.n

                        def fve2(v, tn=tn):
                            v.wait_ge(cT.h, tn)
                            v.tensor_copy(kvro[:], psX[:])
                            cV.inc(v.engine_nop())
                        on("vector", fve2)
                        vn2 = cV.n

                        def ftb2(t, o=o, fi=fi, vn2=vn2):
                            t.wait_ge(cV.h, vn2)
                            cT.inc(t.matmul(psW0[:, 0:512],
                                            itall[:, 128 * o:128 * o + 128],
                                            expFA[:, 512 * fi:512 * fi + 512],
                                            start=True, stop=True))
                        on("tensor", ftb2)
                        tn2 = cT.n

                        def ffx(v, o=o, tn2=tn2):
                            v.wait_ge(cT.h, tn2)
                            v.wait_ge(cS.h, pairs_scalar)
                            v.tensor_mul(fxt[:], psW0[:, 0:512], kvro[:])
                            VE(v)
                            v.tensor_add(attnC[:, 512 * o:512 * o + 512],
                                         attnC[:, 512 * o:512 * o + 512], fxt[:])
                            cV.inc(v.engine_nop())
                        on("vector", ffx)

                for o in range(8):
                    vprev = cV.n

                    def ftr2(t, o=o, rcn=rcn, vprev=vprev):
                        t.wait_ge(cV.h, max(rcn, vprev))
                        cT.inc(t.matmul(psX[:], selpair[:, 128 * o:128 * o + 128],
                                        recip16[:], start=True, stop=True))
                    on("tensor", ftr2)
                    tn = cT.n

                    def fap(v, o=o, tn=tn):
                        v.wait_ge(cT.h, tn)
                        v.tensor_mul(attnpre[:, 512 * o:512 * o + 512],
                                     attnC[:, 512 * o:512 * o + 512], psX[:])
                        cV.inc(v.engine_nop())
                    on("vector", fap)
                apn = cV.n

                wring_guard = [cT.n, cT.n]
                for grp in range(2):
                    for o in range(8):
                        def flp(sync, o=o, grp=grp, wg=wring_guard[o % 2]):
                            sync.wait_ge(cT.h, wg)
                            dS.dma(sync.dma_start(
                                out=wring[:, 1024 * (o % 2):1024 * (o % 2) + 1024],
                                in_=wprojT_s[o]))
                        on("sync", flp)
                        ldw = dS.n

                        def ftp(t, o=o, grp=grp, ldw=ldw, apn=apn):
                            t.wait_ge(dS.h, ldw)
                            t.wait_ge(cV.h, apn)
                            for mtl in range(2):
                                mt = 2 * grp + mtl
                                pdst = psW0 if mtl == 0 else psW1
                                lhsT = attnpre[:, 512 * o + 128 * mt:512 * o + 128 * mt + 128]
                                _li = t.matmul(pdst[:, 0:512], lhsT,
                                         wring[:, 1024 * (o % 2):1024 * (o % 2) + 512],
                                         start=(o == 0), stop=(o == 7))
                                _li = t.matmul(pdst[:, 512:1024], lhsT,
                                         wring[:, 1024 * (o % 2) + 512:1024 * (o % 2) + 1024],
                                         start=(o == 0), stop=(o == 7))
                            cT.inc(_li)
                        on("tensor", ftp)
                        wring_guard[o % 2] = cT.n
                    tn = cT.n
                    for mtl in range(2):
                        mt = 2 * grp + mtl

                        def flu(sync, mt=mt, c=c, _vp=cV.n, _dp=dS.n):
                            sync.wait_ge(cV.h, _vp)
                            sync.wait_ge(dS.h, _dp)
                            dS.dma(sync.dma_start(out=u1t[:], in_=u_s[4 * c + mt]))
                        on("sync", flu)
                        ldu = dS.n

                        def fu1(v, mtl=mtl, tn=tn, ldu=ldu):
                            v.wait_ge(cT.h, tn)
                            v.wait_ge(dS.h, ldu)
                            pdst = psW0 if mtl == 0 else psW1
                            v.tensor_add(u1t[:], pdst[:], u1t[:])
                            cV.inc(v.engine_nop())
                        on("vector", fu1)
                        vn = cV.n

                        def fu1s(sync, mt=mt, c=c, vn=vn):
                            sync.wait_ge(cV.h, vn)
                            dS.dma(sync.dma_start(out=u1_s[4 * c + mt], in_=u1t[:]))
                        on("sync", fu1s)

    def s3(b):
        with contextlib.ExitStack() as ps:
            def psb(name, shape, dt=F32):
                return ps.enter_context(nc.sbuf_tensor(_un(name), shape, dt))
            h2T = psb("s3_h2T", [128, 8 * 1024], BF16)
            gT = psb("s3_gT", [128, 32 * 1024], BF16)
            wf1r = psb("s3_wf1r", [128, 2 * 1024], BF16)
            wf2r = psb("s3_wf2r", [128, 2 * 512], BF16)
            xt2 = psb("s3_xt2", [128, 2 * 1024], F32)
            sqb = psb("s3_sqb", [128, 1024], F32)
            h2f = psb("s3_h2f", [128, 1024], F32)
            u2t = psb("s3_u2t", [128, 1024], F32)
            xat = psb("s3_xat", [128, 512], F32)
            ps3 = [ps.enter_context(nc.psum_tensor(_un(f"s3_ps{i}"), [128, 1024], F32))
                   for i in range(4)]

            for half, row_hbm in [(0, ln2g), (1, ln2b)]:
                def flr(sync, row_hbm=row_hbm, _tp=cT.n):
                    sync.wait_ge(cT.h, _tp)
                    dS.dma(sync.dma_start(out=lnrow[:], in_=row_hbm[None, :]))
                on("sync", flr)
                ldr = dS.n

                def ftg(t, half=half, ldr=ldr, _vp=cV.n):
                    t.wait_ge(dS.h, ldr)
                    t.wait_ge(cV.h, _vp)
                    pdst = ps3[0] if half == 0 else ps3[1]
                    for n in range(2):
                        cT.inc(t.matmul(
                            pdst[:, 512 * n:512 * n + 512], onesf[:],
                            lnrow[0:1, 512 * n:512 * n + 512],
                            start=True, stop=True))
                on("tensor", ftg)
            tn = cT.n

            def fvg(v, tn=tn):
                v.wait_ge(cT.h, tn)
                v.tensor_copy(gb_bc[:, 0:D], ps3[0][:])
                v.tensor_copy(gb_bc[:, D:2 * D], ps3[1][:])
                cV.inc(v.engine_nop())
            on("vector", fvg)
            gbn = cV.n

            for hf in range(2):
                ev_done = {}
                for i in range(8):
                    ti = 8 * hf + i
                    prevev = ev_done.get(i - 2, 0)

                    def fs(sync, ti=ti, i=i, prevev=prevev):
                        sync.wait_ge(cV.h, prevev)
                        dS.dma(sync.dma_start(out=xt2[:, D * (i % 2):D * (i % 2) + D],
                                              in_=u1_s[ti]))
                    on("sync", fs)
                    ldn = dS.n
                    xs = xt2[:, D * (i % 2):D * (i % 2) + D]

                    def fv1(v, xs=xs, ldn=ldn, i=i, _sn=cS.n):
                        v.wait_ge(dS.h, ldn)
                        v.wait_ge(cS.h, _sn)
                        v.reduce_sum(stats[:, 0:1], xs, AX)
                        v.tensor_mul(sqb[:], xs, xs)
                        v.reduce_sum(stats[:, 1:2], sqb[:], AX)
                        v.tensor_scalar_mul(stats[:, 2:3], stats[:, 0:1], float(1.0 / D))
                        v.tensor_scalar_mul(stats[:, 3:4], stats[:, 1:2], float(1.0 / D))
                        VE(v)
                        v.tensor_mul(stats[:, 4:5], stats[:, 2:3], stats[:, 2:3])
                        VE(v)
                        v.tensor_sub(stats[:, 3:4], stats[:, 3:4], stats[:, 4:5])
                        cV.inc(v.engine_nop())
                    on("vector", fv1)
                    vn = cV.n

                    def fs1(s, vn=vn):
                        s.wait_ge(cV.h, vn)
                        _li = s.activation(stats[:, 4:5], stats[:, 3:4], ACT.Ln, bias=float(EPS))
                        _li = s.activation(stats[:, 5:6], stats[:, 4:5], ACT.Exp, scale=-0.5)
                        cS.inc(_li)
                    on("scalar", fs1)
                    sn = cS.n

                    def fv2(v, sn=sn):
                        v.wait_ge(cS.h, sn)
                        v.scalar_tensor_tensor(stats[:, 6:7], stats[:, 2:3], -1.0,
                                               stats[:, 5:6], ALU.mult, ALU.mult)
                        cV.inc(v.engine_nop())
                    on("vector", fv2)
                    vn = cV.n

                    def fv3(v, xs=xs, vn=vn, gbn=gbn, _tp=cT.n):
                        v.wait_ge(cV.h, max(vn, gbn))
                        v.wait_ge(cT.h, _tp)
                        v.tensor_scalar(h2f[:], xs, stats[:, 5:6], stats[:, 6:7],
                                        ALU.mult, ALU.add)
                        v.tensor_mul(h2f[:], h2f[:], gb_bc[:, 0:D])
                        v.tensor_add(h2f[:], h2f[:], gb_bc[:, D:2 * D])
                        cV.inc(v.engine_nop())
                    on("vector", fv3)
                    vn = cV.n
                    prev_ev1 = ev_done.get(i - 1, 0)

                    def ftr(t, vn=vn, prev_ev1=prev_ev1):
                        t.wait_ge(cV.h, max(vn, prev_ev1))
                        for j in range(8):
                            _li = t.transpose(ps3[2][:, 128 * j:128 * j + 128],
                                        h2f[:, 128 * j:128 * j + 128], identf[:])
                        cT.inc(_li)
                    on("tensor", ftr)
                    tn = cT.n

                    def fev(v, i=i, tn=tn):
                        v.wait_ge(cT.h, tn)
                        v.tensor_copy(
                            h2T[:].rearrange("p (j t) -> p j t", j=8)[:, :, 128 * i:128 * i + 128],
                            ps3[2][:].rearrange("p (j t) -> p j t", j=8))
                        cV.inc(v.engine_nop())
                        ev_done[i] = cV.n
                    on("vector", fev)
                h2T_done = cV.n

                for o in range(32):
                    scn = cS.n
                    cgn = cG.n

                    def flw(g, o=o, scn=scn):
                        g.wait_ge(cS.h, scn)
                        dG.dma(g.dma_start(
                            out=wf1r[:, 1024 * (o % 2):1024 * (o % 2) + 1024]
                            .rearrange("p (k n) -> p k n", k=8),
                            in_=wff1[:, 128 * o:128 * o + 128]
                            .rearrange("(k p) n -> p k n", p=128)))
                    on("gpsimd", flw)
                    ldw = dG.n
                    scn = cS.n

                    def ft1(t, o=o, ldw=ldw, scn=scn, h2T_done=h2T_done):
                        t.wait_ge(dG.h, ldw)
                        t.wait_ge(cV.h, h2T_done)
                        t.wait_ge(cS.h, scn)
                        pdst = ps3[o % 2]
                        for k in range(8):
                            lhsT = wf1r[:, 1024 * (o % 2) + 128 * k:
                                        1024 * (o % 2) + 128 * k + 128]
                            for n in range(2):
                                _li = t.matmul(pdst[:, 512 * n:512 * n + 512], lhsT,
                                         h2T[:, 1024 * k + 512 * n:1024 * k + 512 * n + 512],
                                         start=(k == 0), stop=(k == 7))
                        cT.inc(_li)
                    on("tensor", ft1)
                    tn = cT.n

                    def fg1(s, o=o, tn=tn):
                        s.wait_ge(cT.h, tn)
                        _li = s.activation(gT[:, 1024 * o:1024 * o + 1024],
                                     ps3[o % 2][:], ACT.Gelu)
                        cS.inc(_li)
                    on("scalar", fg1)
                gdone = cS.n

                u2n = {}
                wf2_guard = [cT.n, cT.n]
                for n in range(2):
                    for cc in range(32):
                        def flw2(g, cc=cc, n=n, wg=wf2_guard[cc % 2]):
                            g.wait_ge(cT.h, wg)
                            dG.dma(g.dma_start(
                                out=wf2r[:, 512 * (cc % 2):512 * (cc % 2) + 512],
                                in_=wff2[128 * cc:128 * cc + 128, 512 * n:512 * n + 512]))
                        on("gpsimd", flw2)
                        ldw = dG.n

                        def ft2(t, cc=cc, n=n, ldw=ldw, gdone=gdone, _vp=cV.n):
                            t.wait_ge(dG.h, ldw)
                            t.wait_ge(cS.h, gdone)
                            if cc == 0:
                                t.wait_ge(cV.h, _vp)
                            for tile in range(8):
                                _li = t.matmul(ps3[tile // 2][:, 512 * (tile % 2):512 * (tile % 2) + 512],
                                         gT[:, 1024 * cc + 128 * tile:1024 * cc + 128 * tile + 128],
                                         wf2r[:, 512 * (cc % 2):512 * (cc % 2) + 512],
                                         start=(cc == 0), stop=(cc == 31))
                            cT.inc(_li)
                        on("tensor", ft2)
                        wf2_guard[cc % 2] = cT.n
                    tn = cT.n
                    for tile in range(8):
                        ti = 8 * hf + tile

                        def flu1(sync, ti=ti, n=n, _vp=cV.n):
                            sync.wait_ge(cV.h, _vp)
                            dS.dma(sync.dma_start(
                                out=u2t[:, 0:512],
                                in_=u1_s[ti][:, 512 * n:512 * n + 512]))
                            dS.dma(sync.dma_start(
                                out=u2t[:, 512:1024],
                                in_=x_in[b, 128 * ti:128 * ti + 128,
                                         512 * n:512 * n + 512]))
                        on("sync", flu1)
                        ldn2 = dS.n

                        def fu2b(v, tile=tile, n=n, tn=tn, ldn2=ldn2):
                            v.wait_ge(cT.h, tn)
                            v.wait_ge(dS.h, ldn2)
                            v.tensor_add(u2t[:, 0:512],
                                         ps3[tile // 2][:, 512 * (tile % 2):512 * (tile % 2) + 512],
                                         u2t[:, 0:512])
                            VE(v)
                            v.tensor_add(xat[:, 0:512], u2t[:, 0:512],
                                         u2t[:, 512:1024])
                            cV.inc(v.engine_nop())
                        on("vector", fu2b)
                        vn2 = cV.n

                        def fxs2(sync, ti=ti, n=n, vn2=vn2):
                            sync.wait_ge(cV.h, vn2)
                            dS.dma(sync.dma_start(
                                out=xa_s[ti][:, 512 * n:512 * n + 512],
                                in_=xat[:, 0:512]))
                        on("sync", fxs2)
                    u2n[n] = cV.n

    def s4(b):
        with contextlib.ExitStack() as ps:
            def psb(name, shape, dt=F32):
                return ps.enter_context(nc.sbuf_tensor(_un(name), shape, dt))
            ntT = psb("s4_ntT", [128, 8 * T], BF16)
            xat = psb("s4_xat", [128, 2 * 1024], F32)
            sqb = psb("s4_sqb", [128, 1024], F32)
            vq = psb("s4_vq", [128, 1024], F32)
            nq32 = psb("s4_nq32", [128, 1024], I32)
            xqb = psb("s4_xqb", [128, 1024], BF16)
            m1r = psb("s4_m1r", [128, 2 * 1024], BF16)
            hb = psb("s4_hb", [128, 2048], F32)
            hs2 = psb("s4_hs2", [128, 2048], F32)
            maxacc = psb("s4_maxacc", [128, 2048], F32)
            minacc = psb("s4_minacc", [128, 2048], F32)
            s2bc = psb("s4_s2bc", [128, 2048], F32)
            n2i = psb("s4_n2i", [128, 2048], I32)
            n2b = psb("s4_n2b", [128, 2048], BF16)
            psA = ps.enter_context(nc.psum_tensor(_un("s4_psA"), [128, 1024], F32))
            psB = ps.enter_context(nc.psum_tensor(_un("s4_psB"), [128, 1024], F32))
            psQ = ps.enter_context(nc.psum_tensor(_un("s4_psQ"), [128, 1024], BF16))
            psR = ps.enter_context(nc.psum_tensor(_un("s4_psR"), [128, 512], F32))

            ev_done = {}
            for i in range(NT):
                prevev = ev_done.get(i - 2, 0)

                def fs(sync, i=i, prevev=prevev):
                    sync.wait_ge(cV.h, prevev)
                    dS.dma(sync.dma_start(out=xat[:, D * (i % 2):D * (i % 2) + D],
                                          in_=xa_s[i]))
                on("sync", fs)
                ldn = dS.n
                xs = xat[:, D * (i % 2):D * (i % 2) + D]

                def fv1(v, xs=xs, ldn=ldn, _sn=cS.n):
                    v.wait_ge(dS.h, ldn)
                    v.wait_ge(cS.h, _sn)
                    v.reduce_sum(stats[:, 0:1], xs, AX)
                    v.tensor_mul(sqb[:], xs, xs)
                    v.reduce_sum(stats[:, 1:2], sqb[:], AX)
                    v.tensor_scalar_mul(stats[:, 2:3], stats[:, 0:1], float(1.0 / D))
                    v.tensor_scalar_mul(stats[:, 3:4], stats[:, 1:2], float(1.0 / D))
                    VE(v)
                    v.tensor_mul(stats[:, 4:5], stats[:, 2:3], stats[:, 2:3])
                    VE(v)
                    v.tensor_sub(stats[:, 3:4], stats[:, 3:4], stats[:, 4:5])
                    cV.inc(v.engine_nop())
                on("vector", fv1)
                vn = cV.n

                def fs1(s, vn=vn):
                    s.wait_ge(cV.h, vn)
                    _li = s.activation(stats[:, 4:5], stats[:, 3:4], ACT.Ln, bias=float(EPS))
                    _li = s.activation(stats[:, 5:6], stats[:, 4:5], ACT.Exp, scale=-0.5)
                    cS.inc(_li)
                on("scalar", fs1)
                sn = cS.n

                def fv2(v, sn=sn, i=i, xs=xs):
                    v.wait_ge(cS.h, sn)
                    v.scalar_tensor_tensor(stats[:, 6:7], stats[:, 2:3], -1.0,
                                           stats[:, 5:6], ALU.mult, ALU.mult)
                    cV.inc(v.engine_nop())
                on("vector", fv2)
                vn = cV.n

                def fv3(v, i=i, xs=xs, vn=vn):
                    v.wait_ge(cV.h, vn)
                    v.tensor_scalar(vq[:], xs, stats[:, 5:6], stats[:, 6:7],
                                    ALU.mult, ALU.add)
                    v.reduce_max(stats[:, 7:8], vq[:], AX, apply_absolute_value=True)
                    VE(v)
                    v.tensor_scalar_max(stats[:, 7:8], stats[:, 7:8], 1e-5)
                    VE(v)
                    v.tensor_mul(gam1[:, i:i + 1], stats[:, 7:8], meanw[:, 0:1])
                    v.reciprocal(stats[:, 6:7], stats[:, 7:8])
                    VE(v)
                    v.tensor_scalar_mul(gam1[:, i:i + 1], gam1[:, i:i + 1],
                                        float(1.0 / 127.0))
                    v.tensor_scalar_mul(stats[:, 7:8], stats[:, 6:7], 127.0)
                    cV.inc(v.engine_nop())
                on("vector", fv3)
                vn = cV.n

                def fv4(v, i=i, vn=vn, _tp=cT.n):
                    v.wait_ge(cV.h, vn)
                    v.wait_ge(cT.h, _tp)
                    v.tensor_scalar_mul(sqb[:], vq[:], stats[:, 7:8])
                    v.tensor_copy(nq32[:], sqb[:])
                    v.tensor_scalar_mul(xqb[:], nq32[:], gam1[:, i:i + 1])
                    cV.inc(v.engine_nop())
                on("vector", fv4)
                vn = cV.n
                prev1 = ev_done.get(i - 1, 0)

                def ftr(t, vn=vn, prev1=prev1):
                    t.wait_ge(cV.h, max(vn, prev1))
                    for j in range(8):
                        _li = t.transpose(psQ[:, 128 * j:128 * j + 128],
                                    xqb[:, 128 * j:128 * j + 128], ident16[:])
                    cT.inc(_li)
                on("tensor", ftr)
                tn = cT.n

                def fev(v, i=i, tn=tn):
                    v.wait_ge(cT.h, tn)
                    v.tensor_copy(
                        ntT[:].rearrange("p (j t) -> p j t", j=8)[:, :, 128 * i:128 * i + 128],
                        psQ[:].rearrange("p (j t) -> p j t", j=8))
                    cV.inc(v.engine_nop())
                    ev_done[i] = cV.n
                on("vector", fev)
            nt_done = cV.n

            def fmz(v):
                v.memset(maxacc[:], 0.0)
                v.memset(minacc[:], 0.0)
                cV.inc(v.engine_nop())
            on("vector", fmz)
            for ct in range(32):
                scn = cS.n

                def flm(sc, ct=ct, scn=scn):
                    sc.wait_ge(cS.h, scn)
                    dC.dma(sc.dma_start(
                        out=m1r[:, 1024 * (ct % 2):1024 * (ct % 2) + 1024]
                        .rearrange("p (k c) -> p k c", k=8),
                        in_=m1T_s[ct].rearrange("k p c -> p k c")))
                on("scalar", flm)
                ldm = dC.n
                scn = cS.n

                def ft1(t, ct=ct, ldm=ldm, scn=scn, nt_done=nt_done):
                    t.wait_ge(dC.h, ldm)
                    t.wait_ge(cV.h, nt_done)
                    t.wait_ge(cS.h, scn)
                    for k in range(8):
                        lhsT = m1r[:, 1024 * (ct % 2) + 128 * k:
                                   1024 * (ct % 2) + 128 * k + 128]
                        for n in range(4):
                            pdst = psA if n < 2 else psB
                            _li = t.matmul(pdst[:, 512 * (n % 2):512 * (n % 2) + 512], lhsT,
                                     ntT[:, T * k + 512 * n:T * k + 512 * n + 512],
                                     start=(k == 0), stop=(k == 7))
                    cT.inc(_li)
                on("tensor", ft1)
                tn = cT.n
                vprev = cV.n

                def fhe(s, tn=tn, _vp=cV.n, _gp=cG.n):
                    s.wait_ge(cT.h, tn)
                    s.wait_ge(cV.h, _vp)
                    s.wait_ge(cG.h, _gp)
                    _li = s.copy(hb[:, 0:1024], psA[:])
                    _li = s.copy(hb[:, 1024:2048], psB[:])
                    cS.inc(_li)
                on("scalar", fhe)
                sn = cS.n

                def fcv(v, ct=ct, sn=sn, _dp=dS.n):
                    v.wait_ge(cS.h, sn)
                    v.wait_ge(dS.h, _dp)
                    v.tensor_scalar_mul(hs2[:], hb[:], convw_pp[:, 3 * ct + 1:3 * ct + 2])
                    v.scalar_tensor_tensor(hs2[:, 1:2048], hb[:, 0:2047],
                                           convw_pp[:, 3 * ct:3 * ct + 1],
                                           hs2[:, 1:2048], ALU.mult, ALU.add)
                    cV.inc(v.engine_nop())
                on("vector", fcv)
                vn = cV.n

                def fcg(v, ct=ct, vn=vn):
                    v.scalar_tensor_tensor(hs2[:, 0:2047], hb[:, 1:2048],
                                           convw_pp[:, 3 * ct + 2:3 * ct + 3],
                                           hs2[:, 0:2047], ALU.mult, ALU.add)
                    cV.inc(v.engine_nop())
                on("vector", fcg)
                gn = cV.n

                def fsl(s, ct=ct, gn=gn):
                    s.wait_ge(cV.h, gn)
                    _li = s.activation(hs2[:], hs2[:], ACT.Silu,
                                 bias=convb_pp[:, ct:ct + 1])
                    cS.inc(_li)
                on("scalar", fsl)
                sn = cS.n

                def fmx(v, sn=sn):
                    v.wait_ge(cS.h, sn)
                    v.tensor_tensor(maxacc[:], maxacc[:], hs2[:], ALU.max)
                    v.tensor_tensor(minacc[:], minacc[:], hs2[:], ALU.min)
                    cV.inc(v.engine_nop())
                on("vector", fmx)
                vn = cV.n

                def fhs(sync, ct=ct, vn=vn):
                    sync.wait_ge(cV.h, vn)
                    dS.dma(sync.dma_start(out=h2w_s[ct], in_=hs2[:]))
                on("sync", fhs)
            mx_done = cV.n
            h2w_done = dS.n

            for j in range(16):
                def ftm(t, j=j, mx_done=mx_done, _vp=cV.n):
                    t.wait_ge(cV.h, max(mx_done, _vp))
                    _li = t.transpose(psR[:, 0:128], maxacc[:, 128 * j:128 * j + 128],
                                identf[:])
                    cT.inc(_li)
                on("tensor", ftm)
                tn = cT.n

                def fvm(v, j=j, tn=tn):
                    v.wait_ge(cT.h, tn)
                    v.reduce_max(smallA[:, 0:1], psR[:, 0:128], AX,
                                 apply_absolute_value=True)
                    VE(v)
                    v.tensor_scalar_max(smallA[:, 0:1], smallA[:, 0:1], 1e-5)
                    VE(v)
                    v.tensor_mul(gam2[:, j:j + 1], smallA[:, 0:1], meanw[:, 1:2])
                    v.reciprocal(smallA[:, 1:2], smallA[:, 0:1])
                    VE(v)
                    v.tensor_scalar_mul(gam2[:, j:j + 1], gam2[:, j:j + 1],
                                        float(1.0 / 127.0))
                    v.tensor_scalar_mul(smallB[:, j:j + 1], smallA[:, 1:2], 127.0)
                    cV.inc(v.engine_nop())
                on("vector", fvm)
                vn = cV.n

                def fts2(t, j=j, vn=vn):
                    t.wait_ge(cV.h, vn)
                    cT.inc(t.matmul(psR[0:1, 0:128],
                                    smallB[:, j:j + 1], identf[:],
                                    start=True, stop=True))
                on("tensor", fts2)
                tn = cT.n

                def fvr(v, j=j, tn=tn):
                    v.wait_ge(cT.h, tn)
                    v.tensor_copy(gb_bc[0:1, 128 * j:128 * j + 128], psR[0:1, 0:128])
                    cV.inc(v.engine_nop())
                on("vector", fvr)
            vn = cV.n
            for n4 in range(4):
                def ftb(t, n4=n4, vn=vn):
                    t.wait_ge(cV.h, vn)
                    cT.inc(t.matmul(psA[:, 512 * (n4 % 2):512 * (n4 % 2) + 512]
                                    if n4 < 2 else psB[:, 512 * (n4 % 2):512 * (n4 % 2) + 512],
                                    onesf[:], gb_bc[0:1, 512 * n4:512 * n4 + 512],
                                    start=True, stop=True))
                on("tensor", ftb)
            tn = cT.n

            def fvb(v, tn=tn):
                v.wait_ge(cT.h, tn)
                v.tensor_copy(s2bc[:, 0:1024], psA[:])
                v.tensor_copy(s2bc[:, 1024:2048], psB[:])
                cV.inc(v.engine_nop())
            on("vector", fvb)
            s2bc_done = cV.n

            for ct in range(32):
                vprev = cV.n

                def fhl(sync, ct=ct, vprev=vprev, _hd=h2w_done):
                    sync.wait_ge(cV.h, vprev)
                    sync.wait_ge(dS.h, _hd)
                    dS.dma(sync.dma_start(out=hs2[:], in_=h2w_s[ct]))
                on("sync", fhl)
                ldn = dS.n

                def fn2(v, ct=ct, ldn=ldn, s2bc_done=s2bc_done):
                    v.wait_ge(dS.h, ldn)
                    v.wait_ge(cV.h, s2bc_done)
                    v.tensor_mul(n2i[:], hs2[:], s2bc[:])
                    v.tensor_copy(n2b[:], n2i[:])
                    cV.inc(v.engine_nop())
                on("vector", fn2)
                vn = cV.n

                def fn2s(sync, ct=ct, vn=vn):
                    sync.wait_ge(cV.h, vn)
                    dS.dma(sync.dma_start(out=n2_s[ct], in_=n2b[:]))
                on("sync", fn2s)

        with contextlib.ExitStack() as ps:
            def psb(name, shape, dt=F32):
                return ps.enter_context(nc.sbuf_tensor(_un(name), shape, dt))
            m2res = psb("s4_m2res", [128, 32 * 1024], BF16)
            n2g = psb("s4_n2g", [128, 32 * 512], BF16)
            xar = psb("s4_xar", [128, 2 * 1024], F32)
            otb = psb("s4_otb", [128, 2 * 1024], F32)
            ps4 = [ps.enter_context(nc.psum_tensor(_un(f"s4b_ps{i}"), [128, 1024], F32))
                   for i in range(4)]

            def flm2(sc):
                dC.dma(sc.dma_start(
                    out=m2res[:].rearrange("p (c o) -> p c o", c=32),
                    in_=m2T_s.rearrange("c p o -> p c o")))
            on("scalar", flm2)
            ldm2 = dC.n

            for tg in range(4):
                vprev = cV.n

                def fln(sync, tg=tg, vprev=vprev):
                    sync.wait_ge(cV.h, vprev)
                    dS.dma(sync.dma_start(
                        out=n2g[:].rearrange("p (c t) -> p c t", c=32),
                        in_=n2_s[:, :, 512 * tg:512 * tg + 512]
                        .rearrange("c p t -> p c t")))
                    for tl in range(2):
                        dS.dma(sync.dma_start(
                            out=xar[:, 1024 * tl:1024 * tl + 1024],
                            in_=xa_s[4 * tg + tl]))
                on("sync", fln)
                ldn = dS.n
                vprev = cV.n

                def ftb2(t, tg=tg, ldn=ldn, ldm2=ldm2, vprev=vprev):
                    t.wait_ge(dS.h, ldn)
                    t.wait_ge(dC.h, ldm2)
                    t.wait_ge(cV.h, vprev)
                    for cc in range(32):
                        for tile in range(4):
                            lhsT = n2g[:, 512 * cc + 128 * tile:512 * cc + 128 * tile + 128]
                            for n in range(2):
                                _li = t.matmul(ps4[tile][:, 512 * n:512 * n + 512], lhsT,
                                         m2res[:, 1024 * cc + 512 * n:1024 * cc + 512 * n + 512],
                                         start=(cc == 0), stop=(cc == 31))
                    cT.inc(_li)
                on("tensor", ftb2)
                tn = cT.n

                for tile in range(4):
                    tt = 4 * tg + tile

                    def fxl(sync, tt=tt, tile=tile, tn=tn):
                        if tile >= 2:
                            sync.wait_ge(cV.h, cV.n)
                            dS.dma(sync.dma_start(
                                out=xar[:, 1024 * (tile % 2):1024 * (tile % 2) + 1024],
                                in_=xa_s[tt]))
                    on("sync", fxl)
                    ldx = dS.n

                    def fo(v, tile=tile, tt=tt, tn=tn, ldx=ldx):
                        v.wait_ge(cT.h, tn)
                        v.wait_ge(dS.h, ldx)
                        for n in range(2):
                            v.scalar_tensor_tensor(
                                otb[:, 1024 * (tile % 2) + 512 * n:
                                    1024 * (tile % 2) + 512 * n + 512],
                                ps4[tile][:, 512 * n:512 * n + 512],
                                gam2[:, tt:tt + 1],
                                xar[:, 1024 * (tile % 2) + 512 * n:
                                    1024 * (tile % 2) + 512 * n + 512],
                                ALU.mult, ALU.add)
                        cV.inc(v.engine_nop())
                    on("vector", fo)
                    vn = cV.n

                    def fos(sync, tt=tt, tile=tile, b=b, vn=vn):
                        sync.wait_ge(cV.h, vn)
                        dS.dma(sync.dma_start(
                            out=out_d[b, 128 * tt:128 * tt + 128, :],
                            in_=otb[:, 1024 * (tile % 2):1024 * (tile % 2) + 1024]))
                    on("sync", fos)

    import os
    _lim = os.environ.get("BASSK_PHASES", "all")
    _order = ["sc", "s0", "s1", "s2", "s3", "s4"]
    _upto = len(_order) if _lim == "all" else _order.index(_lim) + 1

    s0_consts()
    if _upto >= 2:
        s0_wprep()
    barrier()

    with contextlib.ExitStack() as bigs:
        ktT = bigs.enter_context(nc.sbuf_tensor("ktT", [128, 8 * T], BF16))
        vbuf = bigs.enter_context(nc.sbuf_tensor("vbuf", [128, NT * H * 65], BF16))
        for b in range(BL):
            if _upto >= 3:
                s1(b, ktT, vbuf)
                barrier()
            if _upto >= 4:
                s2(b, ktT, vbuf)
                barrier()
            if _upto >= 5:
                s3(b)
                barrier()
            if _upto >= 6:
                s4(b)
                barrier()
        _sbdump = (ktT, vbuf)

    _dump = os.environ.get("BASSK_DUMP", "")
    if _dump:
        _dmap = {"u_s": u_s, "qT_s": qT_s, "fT_s": fT_s, "u1_s": u1_s,
                 "xa_s": xa_s, "h2w_s": h2w_s, "n2_s": n2_s,
                 "wqkvfT_s": wqkvfT_s, "m1T_s": m1T_s, "m2T_s": m2T_s,
                 "wprojT_s": wprojT_s, "dct_s": dct_s,
                 "ktT": _sbdump[0], "vbuf": _sbdump[1]}
        _t = _dmap[_dump]
        _n = 1
        for _d in _t.shape[:-1]:
            _n *= _d
        _w = _t.shape[-1]
        _rows = min(_n, (BL * T * D) // _w)
        _flat_in = _t.rearrange(" ".join(f"d{i}" for i in range(len(_t.shape)))
                                + " -> (" + " ".join(f"d{i}" for i in range(len(_t.shape) - 1))
                                + f") d{len(_t.shape) - 1}")
        _flat_out = out_d.rearrange("b t d -> (b t d)")[0:_rows * _w].rearrange(
            "(r w) -> r w", w=_w)

        def fdmp(g):
            for ctr in [cT, cV, cS, cG]:
                g.wait_ge(ctr.h, ctr.n)
            g.wait_ge(dS.h, dS.n)
            dG.dma(g.dma_start(out=_flat_out[0:_rows, :], in_=_flat_in[0:_rows, :]))
        on("gpsimd", fdmp)

    def fin(sync):
        sync.wait_ge(dS.h, dS.n)
        sync.wait_ge(dG.h, dG.n)
        sync.wait_ge(dC.h, dC.n)
        sync.wait_ge(dT.h, dT.n)
    on("sync", fin)

    es.close()
    return nc


_NC_CACHE = []


def kernel(**inputs):
    x = np.ascontiguousarray(inputs["L0_attn_x"])
    weights = {
        "wqkvf": np.ascontiguousarray(inputs["L0_attn_W_qkvf"]),
        "wconv": np.ascontiguousarray(inputs["L0_attn_W_conv"]),
        "wproj": np.ascontiguousarray(inputs["L0_attn_W_proj"]),
        "wff1": np.ascontiguousarray(inputs["L0_attn_W_ff1"]),
        "wff2": np.ascontiguousarray(inputs["L0_attn_W_ff2"]),
        "ln1g": np.ascontiguousarray(inputs["L0_attn_ln1_g"]),
        "ln1b": np.ascontiguousarray(inputs["L0_attn_ln1_b"]),
        "ln2g": np.ascontiguousarray(inputs["L0_attn_ln2_g"]),
        "ln2b": np.ascontiguousarray(inputs["L0_attn_ln2_b"]),
        "w1": np.ascontiguousarray(inputs["L0_ffn_w1"]),
        "convw": np.ascontiguousarray(inputs["L0_ffn_conv_w"]),
        "convb": np.ascontiguousarray(inputs["L0_ffn_conv_b"]),
        "w2": np.ascontiguousarray(inputs["L0_ffn_w2"]),
    }
    if not _NC_CACHE:
        _NC_CACHE.append(build_nc())
    nc = _NC_CACHE[0]
    in_maps = []
    for core in range(8):
        m = {"x": np.ascontiguousarray(x[BL * core:BL * core + BL])}
        m.update(weights)
        in_maps.append(m)
    res = run_bass_kernel_spmd(nc, in_maps, core_ids=list(range(8)))
    return np.concatenate([r["out"] for r in res.results], axis=0)
